# version 10
# baseline (speedup 1.0000x reference)
import numpy as np
from contextlib import ExitStack

import concourse.bass as bass
import concourse.mybir as mybir
from concourse.bass_utils import run_bass_kernel_spmd

F32 = mybir.dt.float32
F32R = mybir.dt.float32r
AF = mybir.ActivationFunctionType
ALU = mybir.AluOpType
AXX = mybir.AxisListType.X

NCORE = 8
D = 2048
FF = 5632
E = 8
R = 16
H = 16
SEQ = 8192
NTOK = SEQ // NCORE
C = 128
NCH = NTOK // C
NTS = 2
TS = 512
DC = D // 128
FC = FF // 128
EPS_LN = 1e-5
EPS_Z = 1e-6

DEBUG = False
STOP_AFTER = None
A_SUB = 99
EF_SUB = 99
T13_N = 99


class SemC:
    def __init__(self, ctx, nc, name):
        self.h = ctx.enter_context(nc.semaphore(name))
        self.n = 0

    def bump(self, k=1):
        self.n += k
        return self.n


class Arena:
    def __init__(self, tensor, size):
        self.t = tensor
        self.size = size
        self.off = 0

    def reset(self):
        self.off = 0

    def take(self, n, rows=128):
        o = self.off
        assert o + n <= self.size, f"arena overflow {o}+{n} > {self.size}"
        self.off += n
        return self.t[0:rows, o:o + n]


STAGE_ORDER = ["Z", "A", "B", "C1", "CC", "C2", "D", "EF"]

def _stage_on(name):
    if STOP_AFTER is None:
        return True
    return STAGE_ORDER.index(name) <= STAGE_ORDER.index(STOP_AFTER)


def build_nc():
    nc = bass.Bass("TRN2", target_bir_lowering=False)

    dI = lambda name, shape, dt=F32: nc.dram_tensor(name, shape, dt, kind="ExternalInput").ap()
    x_in = dI("x", [NTOK, D])
    wq_b = dI("wq_b", [DC, DC, 128, 128])
    wk_b = dI("wk_b", [DC, DC, 128, 128])
    wv_b = dI("wv_b", [DC, DC, 128, 128])
    wo_b = dI("wo_b", [DC, DC, 128, 128])
    bqB = dI("bqB", [128, DC])
    bkB = dI("bkB", [128, DC])
    bvB = dI("bvB", [128, DC])
    boB = dI("boB", [128, DC])
    gatewT = dI("gatewT", [DC, 128, E])
    w1T_b = dI("w1T_b", [FC, D, 128])
    w3T_b = dI("w3T_b", [FC, D, 128])
    w2T_b = dI("w2T_b", [DC, FC, 128, 128])
    a1T_b = dI("a1T_b", [DC, 128, 128])
    a3T_b = dI("a3T_b", [DC, 128, 128])
    a2T_b = dI("a2T_b", [FC, 128, 128])
    b1T_b = dI("b1T_b", [FC, E, R, 128])
    b3T_b = dI("b3T_b", [FC, E, R, 128])
    b2T_b = dI("b2T_b", [DC, E, R, 128])
    identin = dI("identin", [128, 128])
    maskin = dI("maskin", [128, 128])
    selzin = dI("selzin", [H, H * 128])
    sel8in = dI("sel8in", [E, E * 128])
    onecolin = dI("onecolin", [128, 1])
    prefin = dI("prefin", [128, E])

    out_t = nc.dram_tensor("out", [NTOK, D], F32, kind="ExternalOutput").ap()

    skind = "ExternalOutput" if DEBUG else "Internal"
    dS = lambda name, shape, dt=F32R: nc.dram_tensor(name, shape, dt, kind=skind).ap()
    qfT_s = dS("qfT_s", [NCH, H, 128, 128])
    kfT_s = dS("kfT_s", [NCH, H, 128, 128])
    vT_s = dS("vT_s", [NCH, H, 128, 128])
    aoT_s = dS("aoT_s", [NCH, DC, 128, 128])
    x2_s = dS("x2_s", [NTOK, D], F32)
    xl2T_s = dS("xl2T_s", [DC, 128, NTOK])
    wacc_s = dS("wacc_s", [NTS, FC, 128, TS])
    out2_s = dS("out2_s", [NTS, DC, 128, TS])
    xln2dbg_s = dS("xln2dbg_s", [NCH, 128, D])
    xl2dbg2_s = dS("xl2dbg2_s", [NCH, 128, D])
    ffdbg_s = dS("ffdbg_s", [8, 128, TS])
    wbdbg_s = dS("wbdbg_s", [128, E * TS])
    stdbg_s = dS("stdbg_s", [2, 128, D + H])
    cc_sz = H * 128 * 128 + H * 128
    cc_in = nc.dram_tensor("cc_in", [1, cc_sz], F32).ap()
    cc_out = nc.dram_tensor("cc_out", [NCORE, cc_sz], F32).ap()

    ctx = ExitStack()
    sb = lambda name, shape, dt: ctx.enter_context(nc.sbuf_tensor(name, shape, dt))

    identr = sb("identr", [128, 128], F32R)
    ident32 = sb("ident32", [128, 128], F32)
    maskut = sb("maskut", [128, 128], F32)
    selz = sb("selz", [H, H * 128], F32R)
    sel8 = sb("sel8", [E, E * 128], F32R)
    onecol = sb("onecol", [128, 1], F32R)
    bq_t = sb("bq_t", [128, DC], F32)
    bk_t = sb("bk_t", [128, DC], F32)
    bv_t = sb("bv_t", [128, DC], F32)
    bo_t = sb("bo_t", [128, DC], F32)
    gw_t = sb("gw_t", [128, DC * E], F32R)
    prefm = sb("prefm", [128, E], F32)
    epsln = sb("epsln", [128, 1], F32)
    zeros8 = sb("zeros8", [128, 8], F32)
    zerosT = sb("zerosT", [128, TS], F32)
    stat = sb("stat", [128, 12], F32)
    tp2 = sb("tp2", [128, 10], F32)
    tp8 = sb("tp8", [128, 40], F32)
    kredc = sb("kredc", [128, H], F32)
    ktot = sb("ktot", [128, 2 * H], F32)
    lgt = sb("lgt", [128, E], F32)
    zrow = sb("zrow", [1, 128], F32)
    vspace = sb("vspace", [128, 8], F32)
    tpeq = sb("tpeq", [128, 16], mybir.dt.int32)
    sspace = sb("sspace", [128, 8], F32)

    ARENA_N = 47000
    UE_N = 4096
    arena_t = sb("arena", [128, ARENA_N], F32R)
    A = Arena(arena_t, ARENA_N - UE_N)
    ueT = arena_t[0:R, ARENA_N - UE_N:ARENA_N]

    PA = ctx.enter_context(nc.psum_tensor("PA", [128, 2048], F32))
    PB = ctx.enter_context(nc.psum_tensor("PB", [128, 2048], F32))

    wd = SemC(ctx, nc, "wd")
    gd = SemC(ctx, nc, "gd")
    tn = SemC(ctx, nc, "tn")
    vc = SemC(ctx, nc, "vc")
    sc = SemC(ctx, nc, "sc")
    gp = SemC(ctx, nc, "gp")
    st = SemC(ctx, nc, "st")

    ctx.enter_context(nc.allow_low_precision("deliberate f32r (TF32-grade) on-chip storage"))
    block = ctx.enter_context(nc.Block())
    SY = block.sync
    TE = block.tensor
    VE = block.vector
    SE = block.scalar
    GE = block.gpsimd

    r32 = lambda ap: ap.bitcast(F32)
    rr = lambda ap: ap.bitcast(F32R)

    def ln_stats(xap32, n_feat, sq_scr):
        @VE
        def _(v):
            v.reduce_sum(stat[:, 0:1], xap32, axis=AXX, negate=True)
            v.tensor_copy(vspace[:], zeros8[:]).then_inc(vc.h, 1)
        vc.bump()

        @SE
        def _(s, tgt=vc.n):
            s.wait_ge(vc.h, tgt)
            s.activation(out=sq_scr, in_=xap32, func=AF.Square, accum_out=stat[:, 1:2])
            s.activation(out=sspace[:], in_=zeros8[:], func=AF.Identity).then_inc(sc.h, 1)
        sc.bump()

        @VE
        def _(v, tgt=sc.n):
            v.wait_ge(sc.h, tgt)
            v.tensor_scalar(out=stat[:, 2:3], in0=stat[:, 0:1], scalar1=1.0 / n_feat, scalar2=None, op0=ALU.mult)
            v.engine_nop()
            v.engine_nop()
            v.engine_nop()
            v.engine_nop()
            v.tensor_tensor(stat[:, 3:4], stat[:, 2:3], stat[:, 2:3], op=ALU.mult)
            v.tensor_scalar(out=stat[:, 4:5], in0=stat[:, 1:2], scalar1=1.0 / n_feat, scalar2=None, op0=ALU.mult)
            v.engine_nop()
            v.engine_nop()
            v.engine_nop()
            v.engine_nop()
            v.tensor_sub(stat[:, 4:5], stat[:, 4:5], stat[:, 3:4])
            v.tensor_copy(vspace[:], zeros8[:]).then_inc(vc.h, 1)
        vc.bump()

        @SE
        def _(s, tgt=vc.n):
            s.wait_ge(vc.h, tgt)
            s.activation(out=stat[:, 5:6], in_=stat[:, 4:5], func=AF.Sqrt, bias=epsln[:])
            s.activation(out=sspace[:], in_=zeros8[:], func=AF.Identity).then_inc(sc.h, 1)
        sc.bump()

        @VE
        def _(v, tgt=sc.n):
            v.wait_ge(sc.h, tgt)
            v.reciprocal(stat[:, 6:7], stat[:, 5:6])
            v.engine_nop()
            v.engine_nop()
            v.engine_nop()
            v.engine_nop()
            v.tensor_tensor(stat[:, 7:8], stat[:, 2:3], stat[:, 6:7], op=ALU.mult)
            v.tensor_copy(vspace[:], zeros8[:]).then_inc(vc.h, 1)
        vc.bump()

    def ln_apply(xap32, outap):
        @SE
        def _(s, tgt=vc.n):
            s.wait_ge(vc.h, tgt)
            s.activation(out=outap, in_=xap32, func=AF.Identity, scale=stat[:, 6:7], bias=stat[:, 7:8])
            s.activation(out=sspace[:], in_=zeros8[:], func=AF.Identity).then_inc(sc.h, 1)
        sc.bump()

    @SY
    def _(e):
        for dst, src, cast in [
            (identr, identin, True), (ident32, identin, False), (maskut, maskin, False),
            (selz, selzin, True), (sel8, sel8in, True), (onecol, onecolin, True),
            (bq_t, bqB, False), (bk_t, bkB, False), (bv_t, bvB, False), (bo_t, boB, False),
            (prefm, prefin, False),
        ]:
            e.dma_start(out=dst[:], in_=(rr(src) if cast else src)).then_inc(wd.h, 16)
            wd.bump(16)
        e.dma_start(out=gw_t[:].rearrange("p (k f) -> p k f", k=DC),
                    in_=rr(gatewT).rearrange("k p f -> p k f")).then_inc(wd.h, 16)
        wd.bump(16)

    @GE
    def _(g):
        g.memset(epsln[:], EPS_LN)
        g.memset(zeros8[:], 0.0)
        g.memset(zerosT[:], 0.0)
        g.memset(ktot[:], 0.0)
        g.sem_inc(gp.h, 1)
    gp.bump()

    A.reset()
    xb = [A.take(D) for _ in range(2)]
    sq_scr = A.take(D)
    xln = A.take(D)
    xlnT = A.take(DC * NTOK)
    wst = [A.take(D) for _ in range(2)]
    m0b = A.take(TS)
    p0b = A.take(TS)
    e0b = A.take(TS)
    qev = [A.take(TS) for _ in range(2)]
    xlnT_v = xlnT.rearrange("p (d t) -> p d t", d=DC)
    assert A.off <= ARENA_N - UE_N

    @VE
    def _(v, tgt=gp.n):
        v.wait_ge(gp.h, tgt)
    @SE
    def _(s, tgt=gp.n):
        s.wait_ge(gp.h, tgt)
    @TE
    def _(t, tgt=wd.n):
        t.wait_ge(wd.h, tgt)

    for c in range(NCH if _stage_on('A') else 0):
        @SY
        def _(e, c=c, tgt=sc.n):
            e.wait_ge(sc.h, tgt)
            e.dma_start(out=xb[c % 2], in_=rr(x_in[c * 128:(c + 1) * 128, :])).then_inc(wd.h, 16)
        wd.bump(16)

        @VE
        def _(v, tgt=wd.n):
            v.wait_ge(wd.h, tgt)
        @SE
        def _(s, tgt=wd.n):
            s.wait_ge(wd.h, tgt)

        if A_SUB >= 2:
            ln_stats(r32(xb[c % 2]), D, sq_scr)
        if A_SUB >= 3:
            ln_apply(r32(xb[c % 2]), xln)

        if A_SUB >= 4:
            @TE
            def _(t, tgt=sc.n, tv=vc.n):
                t.wait_ge(sc.h, tgt)
                t.wait_ge(vc.h, tv)
                for k in range(DC):
                    _mm = t.transpose(rr(PA[:, k * 128:(k + 1) * 128]), xln[:, k * 128:(k + 1) * 128], identr[:])
                _mm.then_inc(tn.h, 1)
            tn.bump()

        if A_SUB >= 5:
            @VE
            def _(v, c=c, tgt=tn.n):
                v.wait_ge(tn.h, tgt)
                v.tensor_copy(xlnT_v[:, :, c * 128:(c + 1) * 128], rr(PA[:]).rearrange("p (d t) -> p d t", d=DC))
                v.tensor_copy(vspace[:], zeros8[:]).then_inc(vc.h, 1)
            vc.bump()

    projs = [(wq_b, bq_t, qfT_s, True), (wk_b, bk_t, kfT_s, True), (wv_b, bv_t, vT_s, False)]
    wst_use = [tn.n, tn.n]
    for wbl, bias_t, dst_s, is_elu in (projs if _stage_on('B') else []):
        for dq in range(DC):
            @SY
            def _(e, wbl=wbl, dq=dq, tgt=wst_use[dq % 2]):
                e.wait_ge(tn.h, tgt)
                e.dma_start(out=wst[dq % 2].rearrange("p (k f) -> p k f", k=DC),
                            in_=rr(wbl[dq]).rearrange("k p f -> p k f")).then_inc(wd.h, 16)
            wd.bump(16)

            for ts in range(NTS):
                slot = (2 * dq + ts) % 4
                pa_sl = PA[:, slot * 512:(slot + 1) * 512]

                @TE
                def _(t, dq=dq, ts=ts, pa_sl=pa_sl, tgt=wd.n, tv=vc.n, tsc=sc.n):
                    t.wait_ge(wd.h, tgt)
                    t.wait_ge(vc.h, tv)
                    t.wait_ge(sc.h, tsc)
                    for k in range(DC):
                        _mm = t.matmul(pa_sl, wst[dq % 2][:, k * 128:(k + 1) * 128],
                                 xlnT_v[:, k, ts * TS:(ts + 1) * TS],
                                 start=(k == 0), stop=(k == DC - 1))
                    _mm.then_inc(tn.h, 1)
                tn.bump()
                wst_use[dq % 2] = tn.n

                if is_elu:
                    @VE
                    def _(v, pa_sl=pa_sl, dq=dq, bias_t=bias_t, tgt=tn.n):
                        v.wait_ge(tn.h, tgt)
                        v.scalar_tensor_tensor(out=m0b, in0=pa_sl, scalar=bias_t[:, dq:dq + 1],
                                               in1=zerosT[:], op0=ALU.add, op1=ALU.min)
                        v.tensor_copy(vspace[:], zeros8[:]).then_inc(vc.h, 1)
                    vc.bump()

                    @SE
                    def _(s, pa_sl=pa_sl, dq=dq, bias_t=bias_t, tgt=vc.n, ttn=tn.n):
                        s.wait_ge(tn.h, ttn)
                        s.activation(out=p0b, in_=pa_sl, func=AF.Relu, bias=bias_t[:, dq:dq + 1])
                        s.wait_ge(vc.h, tgt)
                        s.activation(out=e0b, in_=r32(m0b), func=AF.Exp)
                        s.activation(out=sspace[:], in_=zeros8[:], func=AF.Identity).then_inc(sc.h, 1)
                    sc.bump()

                    @VE
                    def _(v, ts=ts, tgt=sc.n, tg=gd.n):
                        v.wait_ge(sc.h, tgt)
                        v.wait_ge(gd.h, tg)
                        v.tensor_add(qev[ts % 2], r32(e0b), r32(p0b))
                        v.tensor_copy(vspace[:], zeros8[:]).then_inc(vc.h, 1)
                    vc.bump()
                    evac_sem, evac_tgt = vc, vc.n
                else:
                    @SE
                    def _(s, pa_sl=pa_sl, dq=dq, ts=ts, bias_t=bias_t, tgt=tn.n, tg=gd.n):
                        s.wait_ge(tn.h, tgt)
                        s.wait_ge(gd.h, tg)
                        s.activation(out=qev[ts % 2], in_=pa_sl, func=AF.Identity,
                                     bias=bias_t[:, dq:dq + 1])
                        s.activation(out=sspace[:], in_=zeros8[:], func=AF.Identity).then_inc(sc.h, 1)
                    sc.bump()
                    evac_sem, evac_tgt = sc, sc.n

                @GE
                def _(g, tgt=evac_tgt, sh=evac_sem.h, dq=dq, ts=ts, dst_s=dst_s):
                    g.wait_ge(sh, tgt)
                    g.dma_start(out=dst_s[ts * 4:(ts + 1) * 4, dq].rearrange("c p f -> p c f"),
                                in_=qev[ts % 2].rearrange("p (c f) -> p c f", c=4)).then_inc(gd.h, 16)
                gd.bump(16)

    A.reset()
    qft = [A.take(D) for _ in range(2)]
    kft = [A.take(D) for _ in range(2)]
    vtt = [A.take(D) for _ in range(2)]
    kftok = A.take(D)
    vtok = A.take(D)
    Sacc = [A.take(D + H) for _ in range(2)]
    atm = [A.take(128) for _ in range(2)]
    zall = A.take(128, rows=H)
    zbsb = A.take(128)
    aoTsb = A.take(D)
    stot = A.take(D)
    ccld = [A.take(D + H) for _ in range(2)]
    assert A.off <= ARENA_N - UE_N

    @GE
    def _(g, tgt=gd.n):
        g.wait_ge(gd.h, tgt)
        g.sem_inc(gp.h, 1)
    gp.bump()

    @VE
    def _(v, tgt=gp.n):
        v.wait_ge(gp.h, tgt)
        v.tensor_copy(zall, zerosT[0:H, 0:128])
        v.tensor_copy(vspace[:], zeros8[:]).then_inc(vc.h, 1)
    vc.bump()

    def load_chunk_qkv(c, want_q):
        @GE
        def _(g, c=c, tgt=tn.n, tv=vc.n, b=c % 2):
            g.wait_ge(tn.h, tgt)
            g.wait_ge(vc.h, tv)
            if want_q:
                g.dma_start(out=qft[b].rearrange("p (h f) -> p h f", h=H),
                            in_=qfT_s[c].rearrange("h p f -> p h f")).then_inc(gd.h, 16)
            g.dma_start(out=kft[b].rearrange("p (h f) -> p h f", h=H),
                        in_=kfT_s[c].rearrange("h p f -> p h f")).then_inc(gd.h, 16)
            g.dma_start(out=vtt[b].rearrange("p (h f) -> p h f", h=H),
                        in_=vT_s[c].rearrange("h p f -> p h f")).then_inc(gd.h, 16)
        gd.bump(16 * (3 if want_q else 2))

    def transpose_to_tok(srcbuf, dstbuf):
        @TE
        def _(t, srcbuf=srcbuf, tgt=gd.n, tv=vc.n):
            t.wait_ge(gd.h, tgt)
            t.wait_ge(vc.h, tv)
            for h in range(H):
                _mm = t.transpose(rr(PA[:, h * 128:(h + 1) * 128]), srcbuf[:, h * 128:(h + 1) * 128], identr[:])
            _mm.then_inc(tn.h, 1)
        tn.bump()

        @VE
        def _(v, dstbuf=dstbuf, tgt=tn.n, tg=gd.n):
            v.wait_ge(tn.h, tgt)
            v.wait_ge(gd.h, tg)
            v.tensor_copy(dstbuf, rr(PA[:]))
            v.tensor_copy(vspace[:], zeros8[:]).then_inc(vc.h, 1)
        vc.bump()

    @TE
    def _(t, tgt=vc.n):
        t.wait_ge(vc.h, tgt)
        for bk in range(4):
            _mm = t.matmul(PB[:, bk * 512:bk * 512 + 16], ident32[:], zerosT[:, 0:16],
                           start=True, stop=True, skip_group_check=True)
        _mm.then_inc(tn.h, 1)
    tn.bump()

    for c in range(NCH if _stage_on('C1') else 0):
        load_chunk_qkv(c, want_q=False)
        transpose_to_tok(kft[c % 2], kftok)
        transpose_to_tok(vtt[c % 2], vtok)

        @TE
        def _(t, c=c, tgt=vc.n):
            t.wait_ge(vc.h, tgt)
            for h in range(H):
                _mm = t.matmul(PB[:, h * 128:(h + 1) * 128], kftok[:, h * 128:(h + 1) * 128],
                         vtok[:, h * 128:(h + 1) * 128], start=False, stop=(c == NCH - 1),
                         skip_group_check=True)
            _mm.then_inc(tn.h, 1)
        tn.bump()

        @VE
        def _(v, c=c, tgt=gp.n, b=c % 2):
            v.wait_ge(gp.h, tgt)
            for h in range(H):
                v.reduce_sum(kredc[:, h:h + 1], r32(kft[b][:, h * 128:(h + 1) * 128]), axis=AXX)
            v.engine_nop()
            v.engine_nop()
            v.engine_nop()
            v.engine_nop()
            v.tensor_add(ktot[:, ((c + 1) % 2) * H:((c + 1) % 2 + 1) * H],
                         ktot[:, (c % 2) * H:(c % 2 + 1) * H], kredc[:])
            v.tensor_copy(vspace[:], zeros8[:]).then_inc(vc.h, 1)
        vc.bump()

    if _stage_on('CC'):
        @VE
        def _(v, tgt=tn.n):
            v.wait_ge(tn.h, tgt)
            v.tensor_copy(stot, PB[:])
            v.tensor_copy(vspace[:], zeros8[:]).then_inc(vc.h, 1)
        vc.bump()

        @GE
        def _(g, tgt=vc.n, base=gd.n):
            g.wait_ge(vc.h, tgt)
            g.dma_start(out=cc_in[0:1, 0:H * 128 * 128].rearrange("o (p n) -> (o p) n", p=128),
                        in_=r32(stot)).then_inc(gd.h, 16)
            g.dma_start(out=cc_in[0:1, H * 128 * 128:].rearrange("o (p n) -> (o p) n", p=128),
                        in_=ktot[:, (NCH % 2) * H:(NCH % 2 + 1) * H]).then_inc(gd.h, 16)
            g.wait_ge(gd.h, base + 32)
            g.collective_compute(
                "AllGather", ALU.bypass,
                replica_groups=[list(range(NCORE))],
                ins=[cc_in.opt()],
                outs=[cc_out.opt()],
            ).then_inc(gp.h, 1)
        gd.bump(32)
        gp.bump()

        @VE
        def _(v):
            for j in range(4):
                v.tensor_copy(Sacc[0][:, j * 512:(j + 1) * 512], zerosT[:])
            v.tensor_copy(Sacc[0][:, 4 * 512:4 * 512 + H], zerosT[:, 0:H])
            v.tensor_copy(vspace[:], zeros8[:]).then_inc(vc.h, 1)
        vc.bump()

        for cp in range(NCORE):
            @GE
            def _(g, cp=cp, tgt=gp.n, tv=vc.n, b=cp % 2):
                g.wait_ge(gp.h, tgt)
                g.wait_ge(vc.h, tv)
                g.dma_start(out=ccld[b][:, 0:D],
                            in_=rr(cc_out[cp:cp + 1, 0:H * 128 * 128]).rearrange("o (p n) -> (o p) n", p=128)).then_inc(gd.h, 16)
                g.dma_start(out=ccld[b][:, D:D + H],
                            in_=rr(cc_out[cp:cp + 1, H * 128 * 128:]).rearrange("o (p n) -> (o p) n", p=128)).then_inc(gd.h, 16)
            gd.bump(32)

            @VE
            def _(v, cp=cp, tgt=gd.n, b=cp % 2):
                v.wait_ge(gd.h, tgt)
                v.scalar_tensor_tensor(out=Sacc[0], in0=r32(ccld[b]), scalar=prefm[:, cp:cp + 1],
                                       in1=r32(Sacc[0]), op0=ALU.mult, op1=ALU.add)
                v.tensor_copy(vspace[:], zeros8[:]).then_inc(vc.h, 1)
            vc.bump()


    if DEBUG:
        @GE
        def _(g, tgt=vc.n):
            g.wait_ge(vc.h, tgt)
            g.dma_start(out=stdbg_s[0, :, 0:D], in_=r32(stot)).then_inc(gd.h, 16)
            g.dma_start(out=stdbg_s[0, :, D:D + H], in_=ktot[:, (NCH % 2) * H:(NCH % 2 + 1) * H]).then_inc(gd.h, 16)
            g.dma_start(out=stdbg_s[1], in_=r32(Sacc[0])).then_inc(gd.h, 16)
        gd.bump(48)

    for c in range(NCH if _stage_on('C2') else 0):
        cur, nxt = Sacc[c % 2], Sacc[(c + 1) % 2]
        Srun, ksrun = cur[:, 0:D], cur[:, D:D + H]
        Snxt, ksnxt = nxt[:, 0:D], nxt[:, D:D + H]

        load_chunk_qkv(c, want_q=True)
        transpose_to_tok(kft[c % 2], kftok)
        transpose_to_tok(vtt[c % 2], vtok)

        for h in range(H):
            at_sl = PA[:, 0:128] if h % 2 == 0 else PA[:, 512:640]
            io_sl = PA[:, 1024:1152]
            z_sl = PA[0:1, 1536:1664]
            zb_sl = PA[:, 1664:1792]
            kv_sl = PB[:, 0:128] if h % 2 == 0 else PB[:, 512:640]
            qsl = qft[c % 2][:, h * 128:(h + 1) * 128]
            kfsl = kft[c % 2][:, h * 128:(h + 1) * 128]
            ktsl = kftok[:, h * 128:(h + 1) * 128]
            vtsl = vtok[:, h * 128:(h + 1) * 128]
            ab = atm[h % 2]

            @TE
            def _(t, at_sl=at_sl, kfsl=kfsl, qsl=qsl, tgt=vc.n):
                t.wait_ge(vc.h, tgt)
                _mm = t.matmul(at_sl, kfsl, qsl, start=True, stop=True)
                _mm.then_inc(tn.h, 1)
            tn.bump()

            @VE
            def _(v, at_sl=at_sl, ab=ab, tgt=tn.n):
                v.wait_ge(tn.h, tgt)
                v.tensor_tensor(ab, at_sl, maskut[:], op=ALU.mult)
                v.tensor_copy(vspace[:], zeros8[:]).then_inc(vc.h, 1)
            vc.bump()

            @TE
            def _(t, io_sl=io_sl, z_sl=z_sl, kv_sl=kv_sl, h=h, ab=ab,
                  qsl=qsl, vtsl=vtsl, ktsl=ktsl, Srun=Srun, ksrun=ksrun, tgt=vc.n):
                t.wait_ge(vc.h, tgt)
                t.matmul(io_sl, Srun[:, h * 128:(h + 1) * 128], qsl, start=True, stop=False)
                t.matmul(io_sl, vtsl, ab, start=False, stop=True)
                t.matmul(z_sl, ksrun[:, h:h + 1], qsl, start=True, stop=False)
                t.matmul(z_sl, onecol[:], ab, start=False, stop=True)
                _mm = t.matmul(kv_sl, ktsl, vtsl, start=True, stop=True)
                _mm.then_inc(tn.h, 1)
            tn.bump()

            @VE
            def _(v, z_sl=z_sl, h=h, tgt=tn.n):
                v.wait_ge(tn.h, tgt)
                v.tensor_scalar(out=zrow[:], in0=z_sl, scalar1=EPS_Z, scalar2=None, op0=ALU.add)
                v.engine_nop()
                v.engine_nop()
                v.engine_nop()
                v.engine_nop()
                v.reciprocal(zall[0:1, :], zrow[:])
                v.tensor_copy(vspace[:], zeros8[:]).then_inc(vc.h, 1)
            vc.bump()

            @TE
            def _(t, zb_sl=zb_sl, h=h, tgt=vc.n):
                t.wait_ge(vc.h, tgt)
                _mm = t.matmul(zb_sl, selz[:, 0:128], zall[:], start=True, stop=True)
                _mm.then_inc(tn.h, 1)
            tn.bump()

            @VE
            def _(v, zb_sl=zb_sl, io_sl=io_sl, kv_sl=kv_sl, h=h, kfsl=kfsl,
                  Srun=Srun, Snxt=Snxt, ksrun=ksrun, ksnxt=ksnxt, tgt=tn.n):
                v.wait_ge(tn.h, tgt)
                v.tensor_copy(zbsb, zb_sl)
                v.engine_nop()
                v.engine_nop()
                v.engine_nop()
                v.engine_nop()
                v.tensor_tensor(aoTsb[:, h * 128:(h + 1) * 128], io_sl, r32(zbsb), op=ALU.mult)
                v.tensor_add(Snxt[:, h * 128:(h + 1) * 128], r32(Srun[:, h * 128:(h + 1) * 128]), kv_sl)
                v.reduce_sum(kredc[:, h:h + 1], r32(kfsl), axis=AXX)
                v.engine_nop()
                v.engine_nop()
                v.engine_nop()
                v.engine_nop()
                v.tensor_add(ksnxt[:, h:h + 1], r32(ksrun[:, h:h + 1]), kredc[:, h:h + 1])
                v.tensor_copy(vspace[:], zeros8[:]).then_inc(vc.h, 1)
            vc.bump()

        @GE
        def _(g, c=c, tgt=vc.n):
            g.wait_ge(vc.h, tgt)
            g.dma_start(out=aoT_s[c].rearrange("h p f -> p h f"),
                        in_=aoTsb.rearrange("p (h f) -> p h f", h=DC)).then_inc(gd.h, 16)
        gd.bump(16)

    A.reset()
    aot = A.take(DC * TS)
    ybufT = A.take(DC * TS)
    wstD = [A.take(D) for _ in range(2)]
    ytok = A.take(D)
    lnY = A.take(D)
    x2b = A.take(D)
    xln2 = A.take(D)
    sqD = A.take(D)
    xstage = A.take(D)
    xl2stg = A.take(D)
    aot_v = aot.rearrange("p (h t) -> p h t", h=DC)
    assert A.off <= ARENA_N - UE_N

    for ts in range(NTS if _stage_on('D') else 0):
        @GE
        def _(g, ts=ts, tgt=tn.n, tv=vc.n):
            g.wait_ge(tn.h, tgt)
            g.wait_ge(vc.h, tv)
            for fch in range(DC):
                g.dma_start(out=aot_v[:, fch, :].rearrange("p (c f) -> p c f", c=4),
                            in_=aoT_s[ts * 4:(ts + 1) * 4, fch].rearrange("c p f -> p c f")).then_inc(gd.h, 16)
        gd.bump(16 * DC)

        wd_use = [tn.n, tn.n]
        for jc in range(DC):
            @SY
            def _(e, jc=jc, tgt=wd_use[jc % 2]):
                e.wait_ge(tn.h, tgt)
                e.dma_start(out=wstD[jc % 2].rearrange("p (k f) -> p k f", k=DC),
                            in_=rr(wo_b[jc]).rearrange("k p f -> p k f")).then_inc(wd.h, 16)
            wd.bump(16)

            pa_sl = PA[:, (jc % 2) * 512:(jc % 2) * 512 + 512]

            @TE
            def _(t, jc=jc, pa_sl=pa_sl, tgt=wd.n, tg=gd.n, tsc=sc.n):
                t.wait_ge(wd.h, tgt)
                t.wait_ge(gd.h, tg)
                t.wait_ge(sc.h, tsc)
                for k in range(DC):
                    _mm = t.matmul(pa_sl, wstD[jc % 2][:, k * 128:(k + 1) * 128], aot_v[:, k, :],
                             start=(k == 0), stop=(k == DC - 1))
                _mm.then_inc(tn.h, 1)
            tn.bump()
            wd_use[jc % 2] = tn.n

            @SE
            def _(s, jc=jc, pa_sl=pa_sl, tgt=tn.n):
                s.wait_ge(tn.h, tgt)
                s.activation(out=ybufT[:, jc * 512:(jc + 1) * 512], in_=pa_sl, func=AF.Identity,
                             bias=bo_t[:, jc:jc + 1])
                s.activation(out=sspace[:], in_=zeros8[:], func=AF.Identity).then_inc(sc.h, 1)
            sc.bump()

        for tc in range(4):
            gtok = ts * 4 + tc

            @TE
            def _(t, tc=tc, tgt=sc.n, tv=vc.n):
                t.wait_ge(sc.h, tgt)
                t.wait_ge(vc.h, tv)
                for jc in range(DC):
                    _mm = t.transpose(rr(PB[:, jc * 128:(jc + 1) * 128]),
                                ybufT[:, jc * 512 + tc * 128: jc * 512 + (tc + 1) * 128], identr[:])
                _mm.then_inc(tn.h, 1)
            tn.bump()

            @VE
            def _(v, tgt=tn.n):
                v.wait_ge(tn.h, tgt)
                v.tensor_copy(ytok, rr(PB[:]))
                v.tensor_copy(vspace[:], zeros8[:]).then_inc(vc.h, 1)
            vc.bump()

            @SY
            def _(e, gtok=gtok, tgt=vc.n):
                e.wait_ge(vc.h, tgt)
                e.dma_start(out=xstage, in_=rr(x_in[gtok * 128:(gtok + 1) * 128, :])).then_inc(wd.h, 16)
            wd.bump(16)

            ln_stats(r32(ytok), D, sqD)
            ln_apply(r32(ytok), lnY)

            @VE
            def _(v, tgt=sc.n, tw=wd.n):
                v.wait_ge(sc.h, tgt)
                v.wait_ge(wd.h, tw)
                v.tensor_add(x2b, r32(xstage), r32(lnY))
                v.tensor_copy(vspace[:], zeros8[:]).then_inc(vc.h, 1)
            vc.bump()

            @GE
            def _(g, gtok=gtok, tgt=vc.n):
                g.wait_ge(vc.h, tgt)
                g.dma_start(out=x2_s[gtok * 128:(gtok + 1) * 128, :], in_=r32(x2b)).then_inc(gd.h, 16)
            gd.bump(16)

            ln_stats(r32(x2b), D, sqD)
            ln_apply(r32(x2b), xln2)

            @SE
            def _(s):
                _ls = s.activation(out=sqD, in_=r32(x2b), func=AF.Square)
                _ls
                s.activation(out=sspace[:], in_=zeros8[:], func=AF.Identity).then_inc(sc.h, 1)
            sc.bump()

            if DEBUG:
                @GE
                def _(g, gtok=gtok, tgt=sc.n):
                    g.wait_ge(sc.h, tgt)
                    g.dma_start(out=xln2dbg_s[gtok], in_=xln2).then_inc(gd.h, 16)
                gd.bump(16)

            @TE
            def _(t, tgt=sc.n, tv=vc.n):
                t.wait_ge(sc.h, tgt)
                t.wait_ge(vc.h, tv)
                for k in range(DC):
                    _mm = t.transpose(rr(PB[:, k * 128:(k + 1) * 128]), xln2[:, k * 128:(k + 1) * 128], identr[:])
                _mm.then_inc(tn.h, 1)
            tn.bump()

            if DEBUG:
                @VE
                def _(v, tgt=tn.n):
                    v.wait_ge(tn.h, tgt)
                    v.tensor_copy(sqD, rr(PB[:]))
                    v.tensor_copy(vspace[:], zeros8[:]).then_inc(vc.h, 1)
                vc.bump()

                @GE
                def _(g, gtok=gtok, tgt=vc.n):
                    g.wait_ge(vc.h, tgt)
                    g.dma_start(out=xl2dbg2_s[gtok], in_=sqD).then_inc(gd.h, 16)
                gd.bump(16)

            @VE
            def _(v, tgt=tn.n, tg=gd.n):
                v.wait_ge(tn.h, tgt)
                v.wait_ge(gd.h, tg)
                v.tensor_copy(xl2stg, rr(PB[:]))
                v.tensor_copy(vspace[:], zeros8[:]).then_inc(vc.h, 1)
            vc.bump()

            @GE
            def _(g, gtok=gtok, tgt=vc.n):
                g.wait_ge(vc.h, tgt)
                g.dma_start(out=xl2T_s[:, :, gtok * 128:(gtok + 1) * 128].rearrange("d p t -> p d t"),
                            in_=xl2stg.rearrange("p (d t) -> p d t", d=DC)).then_inc(gd.h, 16)
            gd.bump(16)

    A.reset()
    xl2t = A.take(DC * TS)
    a1t = A.take(D)
    a3t = A.take(D)
    w1s = [A.take(D) for _ in range(2)]
    w3s = [A.take(D) for _ in range(2)]
    b1s = [A.take(E * 128, rows=R) for _ in range(2)]
    b3s = [A.take(E * 128, rows=R) for _ in range(2)]
    a2s = [A.take(128) for _ in range(2)]
    base1sb = A.take(TS)
    base3sb = A.take(TS)
    ssb = A.take(TS)
    pwsb = A.take(TS)
    pwwsb = [A.take(TS) for _ in range(2)]
    Wb = A.take(E * TS)
    t1c = [A.take(TS, rows=R) for _ in range(E)]
    t3c = [A.take(TS, rows=R) for _ in range(E)]
    lgE = A.take(TS, rows=E)
    w8b = A.take(E)
    w8T = A.take(TS, rows=E)
    waccsb = [A.take(TS) for _ in range(2)]
    usb = A.take(TS)
    assert A.off <= ARENA_N - UE_N, A.off
    xl2t_v = xl2t.rearrange("p (d t) -> p d t", d=DC)

    A.reset()
    w2sF = A.take(FC * 128)
    b2sF = A.take(E * 128, rows=R)
    waccR = A.take(FC * TS)
    o2ld = A.take(D)
    o2tok = A.take(D)
    lnO = A.take(D)
    x2bF = A.take(D)
    sqF = A.take(D)
    o2st = A.take(TS)
    assert A.off <= ARENA_N - UE_N, A.off
    w2sF_v = w2sF.rearrange("p (k f) -> p k f", k=FC)
    waccR_v = waccR.rearrange("p (k t) -> p k t", k=FC)

    for ts in range(NTS if _stage_on('EF') else 0):
        @GE
        def _(g, ts=ts, tgt=tn.n, tv=vc.n, tsc=sc.n):
            g.wait_ge(tn.h, tgt)
            g.wait_ge(vc.h, tv)
            g.wait_ge(sc.h, tsc)
            for dd in range(DC):
                g.dma_start(out=xl2t_v[:, dd, :],
                            in_=xl2T_s[dd, :, ts * TS:(ts + 1) * TS]).then_inc(gd.h, 16)
        gd.bump(16 * DC)

        @SY
        def _(e, tgt=tn.n):
            e.wait_ge(tn.h, tgt)
            e.dma_start(out=a1t.rearrange("p (k f) -> p k f", k=DC),
                        in_=rr(a1T_b).rearrange("k p f -> p k f")).then_inc(wd.h, 16)
            e.dma_start(out=a3t.rearrange("p (k f) -> p k f", k=DC),
                        in_=rr(a3T_b).rearrange("k p f -> p k f")).then_inc(wd.h, 16)
        wd.bump(32)

        if EF_SUB >= 2:
            @TE
            def _(t, tgt=gd.n, tw=wd.n, tv=vc.n):
                t.wait_ge(gd.h, tgt)
                t.wait_ge(wd.h, tw)
                t.wait_ge(vc.h, tv)
                for k in range(DC):
                    _mm = t.matmul(PB[0:E, 1536:1536 + TS], gw_t[:, k * E:(k + 1) * E], xl2t_v[:, k, :],
                             start=(k == 0), stop=(k == DC - 1))
                _mm.then_inc(tn.h, 1)
            tn.bump()

            @VE
            def _(v, tgt=tn.n):
                v.wait_ge(tn.h, tgt)
                v.tensor_copy(lgE, rr(PB[0:E, 1536:1536 + TS]))
                v.tensor_copy(vspace[:], zeros8[:]).then_inc(vc.h, 1)
            vc.bump()

            for tc in range(4):
                @TE
                def _(t, tc=tc, tgt=vc.n):
                    t.wait_ge(vc.h, tgt)
                    _mm = t.transpose(rr(PB[:, 1024:1024 + E]), lgE[:, tc * 128:(tc + 1) * 128], identr[0:E, 0:E])
                    _mm.then_inc(tn.h, 1)
                tn.bump()

                @VE
                def _(v, tgt=tn.n):
                    v.wait_ge(tn.h, tgt)
                    v.tensor_copy(lgt[:], PB[:, 1024:1024 + E])
                    v.engine_nop()
                    v.engine_nop()
                    v.engine_nop()
                    v.engine_nop()
                    v.reduce_max(tp2[:, 0:1], lgt[:], axis=AXX, negate=True)
                    v.engine_nop()
                    v.engine_nop()
                    v.engine_nop()
                    v.engine_nop()
                    v.scalar_tensor_tensor(out=tp8[:, 0:8], in0=lgt[:], scalar=tp2[:, 0:1],
                                           in1=zeros8[:], op0=ALU.add, op1=ALU.add)
                    v.engine_nop()
                    v.engine_nop()
                    v.engine_nop()
                    v.engine_nop()
                    v.tensor_scalar(out=tpeq[:, 0:8], in0=tp8[:, 0:8], scalar1=0.0, scalar2=None,
                                    op0=ALU.is_equal)
                    v.engine_nop()
                    v.engine_nop()
                    v.engine_nop()
                    v.engine_nop()
                    v.tensor_copy(tp8[:, 8:16], tpeq[:, 0:8])
                    v.engine_nop()
                    v.engine_nop()
                    v.engine_nop()
                    v.engine_nop()
                    v.scalar_tensor_tensor(out=tp8[:, 16:24], in0=tp8[:, 8:16], scalar=-1e30,
                                           in1=lgt[:], op0=ALU.mult, op1=ALU.add)
                    v.engine_nop()
                    v.engine_nop()
                    v.engine_nop()
                    v.engine_nop()
                    v.reduce_max(tp2[:, 1:2], tp8[:, 16:24], axis=AXX, negate=True)
                    v.engine_nop()
                    v.engine_nop()
                    v.engine_nop()
                    v.engine_nop()
                    v.scalar_tensor_tensor(out=tp8[:, 24:32], in0=tp8[:, 16:24], scalar=tp2[:, 1:2],
                                           in1=zeros8[:], op0=ALU.add, op1=ALU.add)
                    v.engine_nop()
                    v.engine_nop()
                    v.engine_nop()
                    v.engine_nop()
                    v.tensor_scalar(out=tpeq[:, 8:16], in0=tp8[:, 24:32], scalar1=0.0, scalar2=None,
                                    op0=ALU.is_equal)
                    v.engine_nop()
                    v.engine_nop()
                    v.engine_nop()
                    v.engine_nop()
                    v.tensor_copy(tp8[:, 32:40], tpeq[:, 8:16])
                    v.tensor_sub(tp2[:, 2:3], tp2[:, 1:2], tp2[:, 0:1])
                    v.tensor_copy(vspace[:], zeros8[:]).then_inc(vc.h, 1)
                vc.bump()

                @SE
                def _(s, tgt=vc.n):
                    s.wait_ge(vc.h, tgt)
                    s.activation(out=tp2[:, 3:4], in_=tp2[:, 2:3], func=AF.Sigmoid)
                    s.activation(out=sspace[:], in_=zeros8[:], func=AF.Identity).then_inc(sc.h, 1)
                sc.bump()

                @VE
                def _(v, tgt=sc.n):
                    v.wait_ge(sc.h, tgt)
                    v.tensor_scalar(out=tp2[:, 4:5], in0=tp2[:, 3:4], scalar1=-1.0, scalar2=1.0,
                                    op0=ALU.mult, op1=ALU.add)
                    v.engine_nop()
                    v.engine_nop()
                    v.engine_nop()
                    v.engine_nop()
                    v.scalar_tensor_tensor(out=tp8[:, 0:8], in0=tp8[:, 8:16], scalar=tp2[:, 3:4],
                                           in1=zeros8[:], op0=ALU.mult, op1=ALU.add)
                    v.engine_nop()
                    v.engine_nop()
                    v.engine_nop()
                    v.engine_nop()
                    v.scalar_tensor_tensor(out=w8b, in0=tp8[:, 32:40], scalar=tp2[:, 4:5],
                                           in1=tp8[:, 0:8], op0=ALU.mult, op1=ALU.add)
                    v.tensor_copy(vspace[:], zeros8[:]).then_inc(vc.h, 1)
                vc.bump()

                @TE
                def _(t, tc=tc, tgt=vc.n):
                    t.wait_ge(vc.h, tgt)
                    _mm = t.transpose(rr(PB[0:E, 1280:1280 + 128]), w8b, identr[:])
                    _mm.then_inc(tn.h, 1)
                tn.bump()

                @VE
                def _(v, tc=tc, tgt=tn.n):
                    v.wait_ge(tn.h, tgt)
                    v.tensor_copy(w8T[:, tc * 128:(tc + 1) * 128], rr(PB[0:E, 1280:1280 + 128]))
                    v.tensor_copy(vspace[:], zeros8[:]).then_inc(vc.h, 1)
                vc.bump()

            for e_ in range(E):
                @TE
                def _(t, e_=e_, tgt=vc.n):
                    t.wait_ge(vc.h, tgt)
                    _mm = t.matmul(PB[:, 1536:1536 + TS], sel8[:, e_ * 128:(e_ + 1) * 128], w8T[:],
                             start=True, stop=True)
                    _mm.then_inc(tn.h, 1)
                tn.bump()

                @VE
                def _(v, e_=e_, tgt=tn.n):
                    v.wait_ge(tn.h, tgt)
                    v.tensor_copy(Wb[:, e_ * TS:(e_ + 1) * TS], PB[:, 1536:1536 + TS])
                    v.tensor_copy(vspace[:], zeros8[:]).then_inc(vc.h, 1)
                vc.bump()


        if DEBUG and ts == 0:
            @GE
            def _(g, tgt=vc.n):
                g.wait_ge(vc.h, tgt)
                g.dma_start(out=wbdbg_s[:], in_=rr(Wb)).then_inc(gd.h, 16)
                g.dma_start(out=ffdbg_s[6, 0:E, :], in_=w8T).then_inc(gd.h, 16)
                g.dma_start(out=ffdbg_s[7, 0:E, :], in_=lgE).then_inc(gd.h, 16)
                g.dma_start(out=ffdbg_s[5, :, 0:40], in_=rr(tp8[:])).then_inc(gd.h, 16)
                g.dma_start(out=ffdbg_s[5, :, 64:74], in_=rr(tp2[:])).then_inc(gd.h, 16)
                g.dma_start(out=ffdbg_s[5, :, 80:88], in_=rr(w8b)).then_inc(gd.h, 16)
            gd.bump(96)

        if EF_SUB >= 3:
            for e_ in range(min(E, T13_N)):
                for (at, tcb) in ((a1t, t1c[e_]), (a3t, t3c[e_])):
                    @TE
                    def _(t, at=at, e_=e_, tgt=vc.n):
                        t.wait_ge(vc.h, tgt)
                        for k in range(DC):
                            _mm = t.matmul(PB[0:R, 1536:1536 + TS],
                                     at[:, k * 128 + e_ * R:k * 128 + (e_ + 1) * R],
                                     xl2t_v[:, k, :], start=(k == 0), stop=(k == DC - 1))
                        _mm.then_inc(tn.h, 1)
                    tn.bump()

                    @VE
                    def _(v, tcb=tcb, tgt=tn.n):
                        v.wait_ge(tn.h, tgt)
                        v.tensor_scalar(out=tcb, in0=PB[0:R, 1536:1536 + TS], scalar1=2.0, scalar2=None,
                                        op0=ALU.mult)
                        v.tensor_copy(vspace[:], zeros8[:]).then_inc(vc.h, 1)
                    vc.bump()

        @TE
        def _(t, tgt=vc.n):
            t.wait_ge(vc.h, tgt)
            _mm = t.matmul(PB[:, 512:528], ident32[:], zerosT[:, 0:16], start=True, stop=True,
                           skip_group_check=True)
            _mm.then_inc(tn.h, 1)
        tn.bump()

        e_use = [tn.n, tn.n]
        for fc in range(FC if EF_SUB >= 4 else 0):
            b = fc % 2

            @SY
            def _(e, fc=fc, b=b, tgt=e_use[b]):
                e.wait_ge(tn.h, tgt)
                e.dma_start(out=w1s[b].rearrange("p (k f) -> p k f", k=DC),
                            in_=rr(w1T_b[fc]).rearrange("(k p) f -> p k f", p=128)).then_inc(wd.h, 16)
                e.dma_start(out=w3s[b].rearrange("p (k f) -> p k f", k=DC),
                            in_=rr(w3T_b[fc]).rearrange("(k p) f -> p k f", p=128)).then_inc(wd.h, 16)
                e.dma_start(out=b1s[b].rearrange("r (e f) -> r e f", e=E),
                            in_=rr(b1T_b[fc]).rearrange("e r f -> r e f")).then_inc(wd.h, 16)
                e.dma_start(out=b3s[b].rearrange("r (e f) -> r e f", e=E),
                            in_=rr(b3T_b[fc]).rearrange("e r f -> r e f")).then_inc(wd.h, 16)
                e.dma_start(out=a2s[b], in_=rr(a2T_b[fc])).then_inc(wd.h, 16)
            wd.bump(80)

            @TE
            def _(t, fc=fc, b=b, tgt=wd.n, tv=vc.n):
                t.wait_ge(wd.h, tgt)
                t.wait_ge(vc.h, tv)
                for k in range(DC):
                    t.matmul(PA[:, 0:512], w1s[b][:, k * 128:(k + 1) * 128], xl2t_v[:, k, :],
                             start=(k == 0), stop=(k == DC - 1))
                for k in range(DC):
                    _mm = t.matmul(PA[:, 512:1024], w3s[b][:, k * 128:(k + 1) * 128], xl2t_v[:, k, :],
                             start=(k == 0), stop=(k == DC - 1))
                _mm.then_inc(tn.h, 1)
            tn.bump()

            @VE
            def _(v, tgt=tn.n):
                v.wait_ge(tn.h, tgt)
                v.tensor_copy(base1sb, rr(PA[:, 0:512]))
                v.tensor_copy(base3sb, rr(PA[:, 512:1024]))
                v.tensor_copy(vspace[:], zeros8[:]).then_inc(vc.h, 1)
            vc.bump()

            h_sc = [sc.n, sc.n]
            h_vc = [vc.n, vc.n]

            for e_ in range(E):
                h1bk = PA[:, 1024:1536] if e_ % 2 == 0 else PB[:, 1024:1536]
                h3bk = PA[:, 1536:2048] if e_ % 2 == 0 else PB[:, 1536:2048]

                @TE
                def _(t, fc=fc, b=b, e_=e_, h1bk=h1bk, h3bk=h3bk, tgt=vc.n, tsc=h_sc[e_ % 2]):
                    t.wait_ge(vc.h, tgt)
                    t.wait_ge(sc.h, tsc)
                    t.matmul(h1bk, identr[:], base1sb, start=True, stop=False,
                             skip_group_check=True)
                    _mm = t.matmul(h1bk, b1s[b][:, e_ * 128:(e_ + 1) * 128], t1c[e_],
                             start=False, stop=True, skip_group_check=True)
                    t.matmul(h3bk, identr[:], base3sb, start=True, stop=False,
                             skip_group_check=True)
                    _mm = t.matmul(h3bk, b3s[b][:, e_ * 128:(e_ + 1) * 128], t3c[e_],
                             start=False, stop=True, skip_group_check=True)
                    if e_ > 0:
                        pbp = pwwsb[(e_ - 1) % 2]
                        _mm = t.matmul(PB[:, 0:512], identr[:], pbp, start=(e_ - 1 == 0), stop=False,
                                       skip_group_check=True)
                        for tc in range(4):
                            sl = ((e_ - 1) * 4 + tc) * R
                            _mm = t.matmul(PB[:, 512 + sl:512 + sl + R],
                                           pbp[:, tc * 128:(tc + 1) * 128], a2s[b][:, (e_ - 1) * R:e_ * R],
                                           start=False, stop=(fc == FC - 1), skip_group_check=True)
                    _mm.then_inc(tn.h, 1)
                tn.bump()

                @SE
                def _(s, h1bk=h1bk, tgt=tn.n):
                    s.wait_ge(tn.h, tgt)
                    s.activation(out=ssb, in_=h1bk, func=AF.Silu)
                    s.activation(out=sspace[:], in_=zeros8[:], func=AF.Identity).then_inc(sc.h, 1)
                sc.bump()

                @VE
                def _(v, e_=e_, h3bk=h3bk, tgt=tn.n, tsc=sc.n, pb=pwwsb[e_ % 2]):
                    v.wait_ge(tn.h, tgt)
                    v.wait_ge(sc.h, tsc)
                    v.tensor_tensor(pwsb, r32(ssb), h3bk, op=ALU.mult)
                    v.tensor_tensor(pb, r32(pwsb), r32(Wb[:, e_ * TS:(e_ + 1) * TS]), op=ALU.mult)
                    v.tensor_copy(vspace[:], zeros8[:]).then_inc(vc.h, 1)
                vc.bump()
                h_sc[e_ % 2] = sc.n
                h_vc[e_ % 2] = vc.n

            @TE
            def _(t, fc=fc, b=b, tgt=vc.n, pb=pwwsb[(E - 1) % 2]):
                t.wait_ge(vc.h, tgt)
                t.matmul(PB[:, 0:512], identr[:], pb, start=False, stop=True,
                         skip_group_check=True)
                for tc in range(4):
                    sl = ((E - 1) * 4 + tc) * R
                    _mm = t.matmul(PB[:, 512 + sl:512 + sl + R],
                             pb[:, tc * 128:(tc + 1) * 128], a2s[b][:, (E - 1) * R:E * R],
                             start=False, stop=(fc == FC - 1), skip_group_check=True)
                _mm.then_inc(tn.h, 1)
            tn.bump()

            if DEBUG and ts == 0 and fc == 0:
                @GE
                def _(g, tgt=vc.n, tsc=sc.n):
                    g.wait_ge(vc.h, tgt)
                    g.wait_ge(sc.h, tsc)
                    g.dma_start(out=ffdbg_s[0], in_=base1sb).then_inc(gd.h, 16)
                    g.dma_start(out=ffdbg_s[1], in_=base3sb).then_inc(gd.h, 16)
                    g.dma_start(out=ffdbg_s[2], in_=ssb).then_inc(gd.h, 16)
                    g.dma_start(out=ffdbg_s[3], in_=pwsb).then_inc(gd.h, 16)
                    g.dma_start(out=ffdbg_s[4], in_=pwwsb[(E - 1) % 2]).then_inc(gd.h, 16)
                    g.dma_start(out=ffdbg_s[5, 0:R, :], in_=t1c[0]).then_inc(gd.h, 16)
                gd.bump(96)

            e_use[b] = tn.n

            @VE
            def _(v, fc=fc, wb=waccsb[fc % 2], tgt=tn.n, tg=gd.n):
                v.wait_ge(tn.h, tgt)
                v.wait_ge(gd.h, tg)
                v.tensor_copy(wb, rr(PB[:, 0:512]))
                v.tensor_copy(vspace[:], zeros8[:]).then_inc(vc.h, 1)
            vc.bump()

            @GE
            def _(g, ts=ts, fc=fc, wb=waccsb[fc % 2], tgt=vc.n):
                g.wait_ge(vc.h, tgt)
                g.dma_start(out=wacc_s[ts, fc], in_=wb).then_inc(gd.h, 16)
            gd.bump(16)

        if EF_SUB >= 5:
            @VE
            def _(v, tgt=tn.n):
                v.wait_ge(tn.h, tgt)
                v.tensor_copy(usb, rr(PB[:, 512:1024]))
                v.tensor_copy(vspace[:], zeros8[:]).then_inc(vc.h, 1)
            vc.bump()

            for e_ in range(E):
                for tc in range(4):
                    slot = 1024 + ((e_ * 4 + tc) % 8) * 128

                    @TE
                    def _(t, e_=e_, tc=tc, slot=slot, tgt=vc.n):
                        t.wait_ge(vc.h, tgt)
                        _mm = t.transpose(rr(PB[0:R, slot:slot + 128]),
                                    usb[:, (e_ * 4 + tc) * R:(e_ * 4 + tc + 1) * R], identr[:])
                        _mm.then_inc(tn.h, 1)
                    tn.bump()

                    @VE
                    def _(v, e_=e_, tc=tc, slot=slot, tgt=tn.n):
                        v.wait_ge(tn.h, tgt)
                        v.tensor_scalar(out=ueT[:, e_ * TS + tc * 128:e_ * TS + (tc + 1) * 128],
                                        in0=rr(PB[0:R, slot:slot + 128]),
                                        scalar1=2.0, scalar2=None, op0=ALU.mult)
                        v.tensor_copy(vspace[:], zeros8[:]).then_inc(vc.h, 1)
                    vc.bump()


        if DEBUG and ts == 0:
            @GE
            def _(g, tgt=vc.n):
                g.wait_ge(vc.h, tgt)
                g.dma_start(out=wbdbg_s[0:R, :], in_=ueT).then_inc(gd.h, 16)
            gd.bump(16)

        if EF_SUB >= 6:
            @GE
            def _(g, ts=ts, tgt=vc.n, tsc=sc.n):
                g.wait_ge(vc.h, tgt)
                g.wait_ge(sc.h, tsc)
                g.dma_start(out=waccR_v, in_=wacc_s[ts].rearrange("k p t -> p k t")).then_inc(gd.h, 16)
            gd.bump(16)

            for dc in range(DC):
                @SY
                def _(e, dc=dc, tgt=tn.n):
                    e.wait_ge(tn.h, tgt)
                    e.dma_start(out=w2sF_v, in_=rr(w2T_b[dc]).rearrange("k p f -> p k f")).then_inc(wd.h, 16)
                    e.dma_start(out=b2sF.rearrange("r (e f) -> r e f", e=E),
                                in_=rr(b2T_b[dc]).rearrange("e r f -> r e f")).then_inc(wd.h, 16)
                wd.bump(32)

                pa_sl = PA[:, (dc % 2) * 512:(dc % 2) * 512 + 512]

                @TE
                def _(t, dc=dc, pa_sl=pa_sl, tgt=wd.n, tg=gd.n, tv=vc.n):
                    t.wait_ge(wd.h, tgt)
                    t.wait_ge(gd.h, tg)
                    t.wait_ge(vc.h, tv)
                    for k in range(FC):
                        t.matmul(pa_sl, w2sF_v[:, k, :], waccR_v[:, k, :],
                                 start=(k == 0), stop=False, skip_group_check=True)
                    for e_ in range(E):
                        _mm = t.matmul(pa_sl, b2sF[:, e_ * 128:(e_ + 1) * 128], ueT[:, e_ * TS:(e_ + 1) * TS],
                                 start=False, stop=(e_ == E - 1), skip_group_check=True)
                    _mm.then_inc(tn.h, 1)
                tn.bump()

                @VE
                def _(v, pa_sl=pa_sl, tgt=tn.n, tg=gd.n):
                    v.wait_ge(tn.h, tgt)
                    v.wait_ge(gd.h, tg)
                    v.tensor_copy(o2st, rr(pa_sl))
                    v.tensor_copy(vspace[:], zeros8[:]).then_inc(vc.h, 1)
                vc.bump()

                @GE
                def _(g, ts=ts, dc=dc, tgt=vc.n):
                    g.wait_ge(vc.h, tgt)
                    g.dma_start(out=out2_s[ts, dc], in_=o2st).then_inc(gd.h, 16)
                gd.bump(16)


        for tc in range(4 if EF_SUB >= 7 else 0):
            gtok = ts * 4 + tc

            @GE
            def _(g, ts=ts, tc=tc, gtok=gtok, tgt=vc.n):
                g.wait_ge(vc.h, tgt)
                g.dma_start(out=o2ld.rearrange("p (d f) -> p d f", d=DC),
                            in_=out2_s[ts, :, :, tc * 128:(tc + 1) * 128].rearrange("d p f -> p d f")).then_inc(gd.h, 16)
                g.dma_start(out=x2bF, in_=rr(x2_s[gtok * 128:(gtok + 1) * 128, :])).then_inc(gd.h, 16)
            gd.bump(32)

            @TE
            def _(t, tgt=gd.n, tv=vc.n, tsc=sc.n):
                t.wait_ge(gd.h, tgt)
                t.wait_ge(vc.h, tv)
                t.wait_ge(sc.h, tsc)
                for dcc in range(DC):
                    _mm = t.transpose(rr(PB[:, dcc * 128:(dcc + 1) * 128]),
                                o2ld[:, dcc * 128:(dcc + 1) * 128], identr[:])
                _mm.then_inc(tn.h, 1)
            tn.bump()

            @VE
            def _(v, tgt=tn.n):
                v.wait_ge(tn.h, tgt)
                v.tensor_copy(o2tok, rr(PB[:]))
                v.tensor_copy(vspace[:], zeros8[:]).then_inc(vc.h, 1)
            vc.bump()

            ln_stats(r32(o2tok), D, sqF)
            ln_apply(r32(o2tok), lnO)

            @VE
            def _(v, tgt=sc.n, tg=gd.n):
                v.wait_ge(sc.h, tgt)
                v.wait_ge(gd.h, tg)
                v.tensor_add(lnO, r32(x2bF), r32(lnO))
                v.tensor_copy(vspace[:], zeros8[:]).then_inc(vc.h, 1)
            vc.bump()

            @GE
            def _(g, gtok=gtok, tgt=vc.n):
                g.wait_ge(vc.h, tgt)
                g.dma_start(out=out_t[gtok * 128:(gtok + 1) * 128, :], in_=r32(lnO)).then_inc(st.h, 16)
            st.bump(16)

    @GE
    def _(g, tgt=st.n):
        g.wait_ge(st.h, tgt)

    ctx.close()
    return nc


def _host_layouts(inp):
    f32 = np.float32
    asf = lambda a: np.ascontiguousarray(np.asarray(a, dtype=f32))
    cc = np.ascontiguousarray
    Wq = asf(inp["L0_attn_Wq"]); Wk = asf(inp["L0_attn_Wk"])
    Wv = asf(inp["L0_attn_Wv"]); Wo = asf(inp["L0_attn_Wo"])
    blk = lambda W: cc(W.reshape(DC, 128, DC, 128).transpose(2, 0, 1, 3))
    selz = np.zeros((H, H * 128), dtype=f32)
    for h in range(H):
        selz[h, h * 128:(h + 1) * 128] = 1.0
    sel8 = np.zeros((E, E * 128), dtype=f32)
    for e_ in range(E):
        sel8[e_, e_ * 128:(e_ + 1) * 128] = 1.0
    return {
        "wq_b": blk(Wq), "wk_b": blk(Wk), "wv_b": blk(Wv), "wo_b": blk(Wo),
        "bqB": cc(asf(inp["L0_attn_bq"]).reshape(DC, 128).T),
        "bkB": cc(asf(inp["L0_attn_bk"]).reshape(DC, 128).T),
        "bvB": cc(asf(inp["L0_attn_bv"]).reshape(DC, 128).T),
        "boB": cc(asf(inp["L0_attn_bo"]).reshape(DC, 128).T),
        "gatewT": cc(asf(inp["L0_ffn_gate_w"]).T.reshape(DC, 128, E)),
        "w1T_b": cc(asf(inp["L0_ffn_w1"]).T.reshape(D, FC, 128).transpose(1, 0, 2)),
        "w3T_b": cc(asf(inp["L0_ffn_w3"]).T.reshape(D, FC, 128).transpose(1, 0, 2)),
        "w2T_b": cc(asf(inp["L0_ffn_w2"]).T.reshape(FC, 128, DC, 128).transpose(2, 0, 1, 3)),
        "a1T_b": cc(asf(inp["L0_ffn_a1"]).transpose(2, 0, 1).reshape(D, 128).reshape(DC, 128, 128)),
        "a3T_b": cc(asf(inp["L0_ffn_a3"]).transpose(2, 0, 1).reshape(D, 128).reshape(DC, 128, 128)),
        "a2T_b": cc(asf(inp["L0_ffn_a2"]).transpose(2, 0, 1).reshape(FF, 128).reshape(FC, 128, 128)),
        "b1T_b": cc(asf(inp["L0_ffn_b1"]).reshape(E, FC, 128, R).transpose(1, 0, 3, 2)),
        "b3T_b": cc(asf(inp["L0_ffn_b3"]).reshape(E, FC, 128, R).transpose(1, 0, 3, 2)),
        "b2T_b": cc(asf(inp["L0_ffn_b2"]).reshape(E, DC, 128, R).transpose(1, 0, 3, 2)),
        "identin": np.eye(128, dtype=f32),
        "maskin": np.triu(np.ones((128, 128), dtype=f32)),
        "selzin": selz,
        "sel8in": sel8,
        "onecolin": np.ones((128, 1), dtype=f32),
    }


def kernel(**inputs):
    shared = _host_layouts(inputs)
    x = np.ascontiguousarray(np.asarray(inputs["L0_attn_x"], dtype=np.float32).reshape(SEQ, D))
    nc = build_nc()
    in_maps = []
    for c in range(NCORE):
        m = dict(shared)
        m["x"] = np.ascontiguousarray(x[c * NTOK:(c + 1) * NTOK])
        pm = np.zeros((128, E), np.float32)
        pm[:, :c] = 1.0
        m["prefin"] = pm
        in_maps.append(m)
    res = run_bass_kernel_spmd(nc, in_maps, core_ids=list(range(NCORE)))
    out = np.concatenate([r["out"] for r in res.results], axis=0)
    return out.reshape(1, SEQ, D)


# version 11 (final)
# speedup vs baseline: 1.3167x; 1.3167x over previous
import numpy as np
from contextlib import ExitStack

import concourse.bass as bass
import concourse.mybir as mybir
from concourse.bass_utils import run_bass_kernel_spmd

F32 = mybir.dt.float32
F32R = mybir.dt.float32r
AF = mybir.ActivationFunctionType
ALU = mybir.AluOpType
AXX = mybir.AxisListType.X

NCORE = 8
D = 2048
FF = 5632
E = 8
R = 16
H = 16
SEQ = 8192
NTOK = SEQ // NCORE
C = 128
NCH = NTOK // C
NTS = 2
TS = 512
DC = D // 128
FC = FF // 128
EPS_LN = 1e-5
EPS_Z = 1e-6

DEBUG = False
STOP_AFTER = None
A_SUB = 99
EF_SUB = 99
T13_N = 99


class SemC:
    def __init__(self, ctx, nc, name):
        self.h = ctx.enter_context(nc.semaphore(name))
        self.n = 0

    def bump(self, k=1):
        self.n += k
        return self.n


class Arena:
    def __init__(self, tensor, size):
        self.t = tensor
        self.size = size
        self.off = 0

    def reset(self):
        self.off = 0

    def take(self, n, rows=128):
        o = self.off
        assert o + n <= self.size, f"arena overflow {o}+{n} > {self.size}"
        self.off += n
        return self.t[0:rows, o:o + n]


STAGE_ORDER = ["Z", "A", "B", "C1", "CC", "C2", "D", "EF"]

def _stage_on(name):
    if STOP_AFTER is None:
        return True
    return STAGE_ORDER.index(name) <= STAGE_ORDER.index(STOP_AFTER)


def build_nc():
    nc = bass.Bass("TRN2", target_bir_lowering=False)

    dI = lambda name, shape, dt=F32: nc.dram_tensor(name, shape, dt, kind="ExternalInput").ap()
    x_in = dI("x", [NTOK, D])
    wq_b = dI("wq_b", [DC, DC, 128, 128])
    wk_b = dI("wk_b", [DC, DC, 128, 128])
    wv_b = dI("wv_b", [DC, DC, 128, 128])
    wo_b = dI("wo_b", [DC, DC, 128, 128])
    bqB = dI("bqB", [128, DC])
    bkB = dI("bkB", [128, DC])
    bvB = dI("bvB", [128, DC])
    boB = dI("boB", [128, DC])
    gatewT = dI("gatewT", [DC, 128, E])
    w1T_b = dI("w1T_b", [FC, D, 128])
    w3T_b = dI("w3T_b", [FC, D, 128])
    w2T_b = dI("w2T_b", [DC, FC, 128, 128])
    a1T_b = dI("a1T_b", [DC, 128, 128])
    a3T_b = dI("a3T_b", [DC, 128, 128])
    a2T_b = dI("a2T_b", [FC, 128, 128])
    b1T_b = dI("b1T_b", [FC, E, R, 128])
    b3T_b = dI("b3T_b", [FC, E, R, 128])
    b2T_b = dI("b2T_b", [DC, E, R, 128])
    identin = dI("identin", [128, 128])
    maskin = dI("maskin", [128, 128])
    selzin = dI("selzin", [H, H * 128])
    sel8in = dI("sel8in", [E, E * 128])
    onecolin = dI("onecolin", [128, 1])
    prefin = dI("prefin", [128, E])

    out_t = nc.dram_tensor("out", [NTOK, D], F32, kind="ExternalOutput").ap()

    skind = "ExternalOutput" if DEBUG else "Internal"
    dS = lambda name, shape, dt=F32R: nc.dram_tensor(name, shape, dt, kind=skind).ap()
    qfT_s = dS("qfT_s", [NCH, H, 128, 128])
    kfT_s = dS("kfT_s", [NCH, H, 128, 128])
    vT_s = dS("vT_s", [NCH, H, 128, 128])
    aoT_s = dS("aoT_s", [NCH, DC, 128, 128])
    x2_s = dS("x2_s", [NTOK, D], F32)
    xl2T_s = dS("xl2T_s", [DC, 128, NTOK])
    wacc_s = dS("wacc_s", [NTS, FC, 128, TS])
    out2_s = dS("out2_s", [NTS, DC, 128, TS])
    xln2dbg_s = dS("xln2dbg_s", [NCH, 128, D])
    xl2dbg2_s = dS("xl2dbg2_s", [NCH, 128, D])
    ffdbg_s = dS("ffdbg_s", [8, 128, TS])
    wbdbg_s = dS("wbdbg_s", [128, E * TS])
    stdbg_s = dS("stdbg_s", [2, 128, D + H])
    cc_sz = H * 128 * 128 + H * 128
    cc_in = nc.dram_tensor("cc_in", [1, cc_sz], F32).ap()
    cc_out = nc.dram_tensor("cc_out", [NCORE, cc_sz], F32).ap()

    ctx = ExitStack()
    sb = lambda name, shape, dt: ctx.enter_context(nc.sbuf_tensor(name, shape, dt))

    identr = sb("identr", [128, 128], F32R)
    ident32 = sb("ident32", [128, 128], F32)
    maskut = sb("maskut", [128, 128], F32)
    selz = sb("selz", [H, H * 128], F32R)
    sel8 = sb("sel8", [E, E * 128], F32R)
    onecol = sb("onecol", [128, 1], F32R)
    bq_t = sb("bq_t", [128, DC], F32)
    bk_t = sb("bk_t", [128, DC], F32)
    bv_t = sb("bv_t", [128, DC], F32)
    bo_t = sb("bo_t", [128, DC], F32)
    gw_t = sb("gw_t", [128, DC * E], F32R)
    prefm = sb("prefm", [128, E], F32)
    epsln = sb("epsln", [128, 1], F32)
    zeros8 = sb("zeros8", [128, 8], F32)
    zerosT = sb("zerosT", [128, TS], F32)
    stat = sb("stat", [128, 12], F32)
    tp2 = sb("tp2", [128, 10], F32)
    tp8 = sb("tp8", [128, 40], F32)
    kredc = sb("kredc", [128, H], F32)
    ktot = sb("ktot", [128, 2 * H], F32)
    lgt = sb("lgt", [128, E], F32)
    zrow = sb("zrow", [1, 128], F32)
    vspace = sb("vspace", [128, 8], F32)
    tpeq = sb("tpeq", [128, 16], mybir.dt.int32)
    sspace = sb("sspace", [128, 8], F32)

    ARENA_N = 47000
    UE_N = 4096
    arena_t = sb("arena", [128, ARENA_N], F32R)
    A = Arena(arena_t, ARENA_N - UE_N)
    ueT = arena_t[0:R, ARENA_N - UE_N:ARENA_N]

    PA = ctx.enter_context(nc.psum_tensor("PA", [128, 2048], F32))
    PB = ctx.enter_context(nc.psum_tensor("PB", [128, 2048], F32))

    wd = SemC(ctx, nc, "wd")
    gd = SemC(ctx, nc, "gd")
    tn = SemC(ctx, nc, "tn")
    vc = SemC(ctx, nc, "vc")
    sc = SemC(ctx, nc, "sc")
    gp = SemC(ctx, nc, "gp")
    st = SemC(ctx, nc, "st")

    ctx.enter_context(nc.allow_low_precision("deliberate f32r (TF32-grade) on-chip storage"))
    block = ctx.enter_context(nc.Block())
    SY = block.sync
    TE = block.tensor
    VE = block.vector
    SE = block.scalar
    GE = block.gpsimd

    r32 = lambda ap: ap.bitcast(F32)
    rr = lambda ap: ap.bitcast(F32R)

    def ln_stats(xap32, n_feat, sq_scr):
        @VE
        def _(v):
            v.reduce_sum(stat[:, 0:1], xap32, axis=AXX, negate=True)
            v.tensor_copy(vspace[:], zeros8[:]).then_inc(vc.h, 1)
        vc.bump()

        @SE
        def _(s, tgt=vc.n):
            s.wait_ge(vc.h, tgt)
            s.activation(out=sq_scr, in_=xap32, func=AF.Square, accum_out=stat[:, 1:2])
            s.activation(out=sspace[:], in_=zeros8[:], func=AF.Identity).then_inc(sc.h, 1)
        sc.bump()

        @VE
        def _(v, tgt=sc.n):
            v.wait_ge(sc.h, tgt)
            v.tensor_scalar(out=stat[:, 2:3], in0=stat[:, 0:1], scalar1=1.0 / n_feat, scalar2=None, op0=ALU.mult)
            v.engine_nop()
            v.engine_nop()
            v.engine_nop()
            v.engine_nop()
            v.tensor_tensor(stat[:, 3:4], stat[:, 2:3], stat[:, 2:3], op=ALU.mult)
            v.tensor_scalar(out=stat[:, 4:5], in0=stat[:, 1:2], scalar1=1.0 / n_feat, scalar2=None, op0=ALU.mult)
            v.engine_nop()
            v.engine_nop()
            v.engine_nop()
            v.engine_nop()
            v.tensor_sub(stat[:, 4:5], stat[:, 4:5], stat[:, 3:4])
            v.tensor_copy(vspace[:], zeros8[:]).then_inc(vc.h, 1)
        vc.bump()

        @SE
        def _(s, tgt=vc.n):
            s.wait_ge(vc.h, tgt)
            s.activation(out=stat[:, 5:6], in_=stat[:, 4:5], func=AF.Sqrt, bias=epsln[:])
            s.activation(out=sspace[:], in_=zeros8[:], func=AF.Identity).then_inc(sc.h, 1)
        sc.bump()

        @VE
        def _(v, tgt=sc.n):
            v.wait_ge(sc.h, tgt)
            v.reciprocal(stat[:, 6:7], stat[:, 5:6])
            v.engine_nop()
            v.engine_nop()
            v.engine_nop()
            v.engine_nop()
            v.tensor_tensor(stat[:, 7:8], stat[:, 2:3], stat[:, 6:7], op=ALU.mult)
            v.tensor_copy(vspace[:], zeros8[:]).then_inc(vc.h, 1)
        vc.bump()

    def ln_apply(xap32, outap):
        @SE
        def _(s, tgt=vc.n):
            s.wait_ge(vc.h, tgt)
            s.activation(out=outap, in_=xap32, func=AF.Identity, scale=stat[:, 6:7], bias=stat[:, 7:8])
            s.activation(out=sspace[:], in_=zeros8[:], func=AF.Identity).then_inc(sc.h, 1)
        sc.bump()

    @SY
    def _(e):
        for dst, src, cast in [
            (identr, identin, True), (ident32, identin, False), (maskut, maskin, False),
            (selz, selzin, True), (sel8, sel8in, True), (onecol, onecolin, True),
            (bq_t, bqB, False), (bk_t, bkB, False), (bv_t, bvB, False), (bo_t, boB, False),
            (prefm, prefin, False),
        ]:
            e.dma_start(out=dst[:], in_=(rr(src) if cast else src)).then_inc(wd.h, 16)
            wd.bump(16)
        e.dma_start(out=gw_t[:].rearrange("p (k f) -> p k f", k=DC),
                    in_=rr(gatewT).rearrange("k p f -> p k f")).then_inc(wd.h, 16)
        wd.bump(16)

    @GE
    def _(g):
        g.memset(epsln[:], EPS_LN)
        g.memset(zeros8[:], 0.0)
        g.memset(zerosT[:], 0.0)
        g.memset(ktot[:], 0.0)
        g.sem_inc(gp.h, 1)
    gp.bump()

    A.reset()
    xb = [A.take(D) for _ in range(2)]
    sq_scr = A.take(D)
    xln = A.take(D)
    xlnT = A.take(DC * NTOK)
    wst = [A.take(D) for _ in range(2)]
    m0b = A.take(TS)
    p0b = A.take(TS)
    e0b = A.take(TS)
    qev = [A.take(TS) for _ in range(2)]
    xlnT_v = xlnT.rearrange("p (d t) -> p d t", d=DC)
    assert A.off <= ARENA_N - UE_N

    @VE
    def _(v, tgt=gp.n):
        v.wait_ge(gp.h, tgt)
    @SE
    def _(s, tgt=gp.n):
        s.wait_ge(gp.h, tgt)
    @TE
    def _(t, tgt=wd.n):
        t.wait_ge(wd.h, tgt)

    for c in range(NCH if _stage_on('A') else 0):
        @SY
        def _(e, c=c, tgt=sc.n):
            e.wait_ge(sc.h, tgt)
            e.dma_start(out=xb[c % 2], in_=rr(x_in[c * 128:(c + 1) * 128, :])).then_inc(wd.h, 16)
        wd.bump(16)

        @VE
        def _(v, tgt=wd.n):
            v.wait_ge(wd.h, tgt)
        @SE
        def _(s, tgt=wd.n):
            s.wait_ge(wd.h, tgt)

        if A_SUB >= 2:
            ln_stats(r32(xb[c % 2]), D, sq_scr)
        if A_SUB >= 3:
            ln_apply(r32(xb[c % 2]), xln)

        if A_SUB >= 4:
            @TE
            def _(t, tgt=sc.n, tv=vc.n):
                t.wait_ge(sc.h, tgt)
                t.wait_ge(vc.h, tv)
                for k in range(DC):
                    _mm = t.transpose(rr(PA[:, k * 128:(k + 1) * 128]), xln[:, k * 128:(k + 1) * 128], identr[:])
                _mm.then_inc(tn.h, 1)
            tn.bump()

        if A_SUB >= 5:
            @VE
            def _(v, c=c, tgt=tn.n):
                v.wait_ge(tn.h, tgt)
                v.tensor_copy(xlnT_v[:, :, c * 128:(c + 1) * 128], rr(PA[:]).rearrange("p (d t) -> p d t", d=DC))
                v.tensor_copy(vspace[:], zeros8[:]).then_inc(vc.h, 1)
            vc.bump()

    projs = [(wq_b, bq_t, qfT_s, True), (wk_b, bk_t, kfT_s, True), (wv_b, bv_t, vT_s, False)]
    wst_use = [tn.n, tn.n]
    for wbl, bias_t, dst_s, is_elu in (projs if _stage_on('B') else []):
        for dq in range(DC):
            @SY
            def _(e, wbl=wbl, dq=dq, tgt=wst_use[dq % 2]):
                e.wait_ge(tn.h, tgt)
                e.dma_start(out=wst[dq % 2].rearrange("p (k f) -> p k f", k=DC),
                            in_=rr(wbl[dq]).rearrange("k p f -> p k f")).then_inc(wd.h, 16)
            wd.bump(16)

            for ts in range(NTS):
                slot = (2 * dq + ts) % 4
                pa_sl = PA[:, slot * 512:(slot + 1) * 512]

                @TE
                def _(t, dq=dq, ts=ts, pa_sl=pa_sl, tgt=wd.n, tv=vc.n, tsc=sc.n):
                    t.wait_ge(wd.h, tgt)
                    t.wait_ge(vc.h, tv)
                    t.wait_ge(sc.h, tsc)
                    for k in range(DC):
                        _mm = t.matmul(pa_sl, wst[dq % 2][:, k * 128:(k + 1) * 128],
                                 xlnT_v[:, k, ts * TS:(ts + 1) * TS],
                                 start=(k == 0), stop=(k == DC - 1))
                    _mm.then_inc(tn.h, 1)
                tn.bump()
                wst_use[dq % 2] = tn.n

                if is_elu:
                    @VE
                    def _(v, pa_sl=pa_sl, dq=dq, bias_t=bias_t, tgt=tn.n):
                        v.wait_ge(tn.h, tgt)
                        v.scalar_tensor_tensor(out=m0b, in0=pa_sl, scalar=bias_t[:, dq:dq + 1],
                                               in1=zerosT[:], op0=ALU.add, op1=ALU.min)
                        v.tensor_copy(vspace[:], zeros8[:]).then_inc(vc.h, 1)
                    vc.bump()

                    @SE
                    def _(s, pa_sl=pa_sl, dq=dq, bias_t=bias_t, tgt=vc.n, ttn=tn.n):
                        s.wait_ge(tn.h, ttn)
                        s.activation(out=p0b, in_=pa_sl, func=AF.Relu, bias=bias_t[:, dq:dq + 1])
                        s.wait_ge(vc.h, tgt)
                        s.activation(out=e0b, in_=r32(m0b), func=AF.Exp)
                        s.activation(out=sspace[:], in_=zeros8[:], func=AF.Identity).then_inc(sc.h, 1)
                    sc.bump()

                    @VE
                    def _(v, ts=ts, tgt=sc.n, tg=gd.n):
                        v.wait_ge(sc.h, tgt)
                        v.wait_ge(gd.h, tg)
                        v.tensor_add(qev[ts % 2], r32(e0b), r32(p0b))
                        v.tensor_copy(vspace[:], zeros8[:]).then_inc(vc.h, 1)
                    vc.bump()
                    evac_sem, evac_tgt = vc, vc.n
                else:
                    @SE
                    def _(s, pa_sl=pa_sl, dq=dq, ts=ts, bias_t=bias_t, tgt=tn.n, tg=gd.n):
                        s.wait_ge(tn.h, tgt)
                        s.wait_ge(gd.h, tg)
                        s.activation(out=qev[ts % 2], in_=pa_sl, func=AF.Identity,
                                     bias=bias_t[:, dq:dq + 1])
                        s.activation(out=sspace[:], in_=zeros8[:], func=AF.Identity).then_inc(sc.h, 1)
                    sc.bump()
                    evac_sem, evac_tgt = sc, sc.n

                @GE
                def _(g, tgt=evac_tgt, sh=evac_sem.h, dq=dq, ts=ts, dst_s=dst_s):
                    g.wait_ge(sh, tgt)
                    g.dma_start(out=dst_s[ts * 4:(ts + 1) * 4, dq].rearrange("c p f -> p c f"),
                                in_=qev[ts % 2].rearrange("p (c f) -> p c f", c=4)).then_inc(gd.h, 16)
                gd.bump(16)

    A.reset()
    qft = [A.take(D) for _ in range(2)]
    kft = [A.take(D) for _ in range(2)]
    vtt = [A.take(D) for _ in range(2)]
    kftok = A.take(D)
    vtok = A.take(D)
    Sacc = [A.take(D + H) for _ in range(2)]
    atm = [A.take(128) for _ in range(2)]
    zall = A.take(128, rows=H)
    zbsb = A.take(128)
    aoTsb = A.take(D)
    stot = A.take(D)
    ccld = [A.take(D + H) for _ in range(2)]
    assert A.off <= ARENA_N - UE_N

    @GE
    def _(g, tgt=gd.n):
        g.wait_ge(gd.h, tgt)
        g.sem_inc(gp.h, 1)
    gp.bump()

    @VE
    def _(v, tgt=gp.n):
        v.wait_ge(gp.h, tgt)
        v.tensor_copy(zall, zerosT[0:H, 0:128])
        v.tensor_copy(vspace[:], zeros8[:]).then_inc(vc.h, 1)
    vc.bump()

    def load_chunk_qkv(c, want_q):
        @GE
        def _(g, c=c, tgt=tn.n, tv=vc.n, b=c % 2):
            g.wait_ge(tn.h, tgt)
            g.wait_ge(vc.h, tv)
            if want_q:
                g.dma_start(out=qft[b].rearrange("p (h f) -> p h f", h=H),
                            in_=qfT_s[c].rearrange("h p f -> p h f")).then_inc(gd.h, 16)
            g.dma_start(out=kft[b].rearrange("p (h f) -> p h f", h=H),
                        in_=kfT_s[c].rearrange("h p f -> p h f")).then_inc(gd.h, 16)
            g.dma_start(out=vtt[b].rearrange("p (h f) -> p h f", h=H),
                        in_=vT_s[c].rearrange("h p f -> p h f")).then_inc(gd.h, 16)
        gd.bump(16 * (3 if want_q else 2))

    def transpose_to_tok(srcbuf, dstbuf):
        @TE
        def _(t, srcbuf=srcbuf, tgt=gd.n, tv=vc.n):
            t.wait_ge(gd.h, tgt)
            t.wait_ge(vc.h, tv)
            for h in range(H):
                _mm = t.transpose(rr(PA[:, h * 128:(h + 1) * 128]), srcbuf[:, h * 128:(h + 1) * 128], identr[:])
            _mm.then_inc(tn.h, 1)
        tn.bump()

        @VE
        def _(v, dstbuf=dstbuf, tgt=tn.n, tg=gd.n):
            v.wait_ge(tn.h, tgt)
            v.wait_ge(gd.h, tg)
            v.tensor_copy(dstbuf, rr(PA[:]))
            v.tensor_copy(vspace[:], zeros8[:]).then_inc(vc.h, 1)
        vc.bump()

    @TE
    def _(t, tgt=vc.n):
        t.wait_ge(vc.h, tgt)
        for bk in range(4):
            _mm = t.matmul(PB[:, bk * 512:bk * 512 + 16], ident32[:], zerosT[:, 0:16],
                           start=True, stop=True, skip_group_check=True)
        _mm.then_inc(tn.h, 1)
    tn.bump()

    for c in range(NCH if _stage_on('C1') else 0):
        load_chunk_qkv(c, want_q=False)
        transpose_to_tok(kft[c % 2], kftok)
        transpose_to_tok(vtt[c % 2], vtok)

        @TE
        def _(t, c=c, tgt=vc.n):
            t.wait_ge(vc.h, tgt)
            for h in range(H):
                _mm = t.matmul(PB[:, h * 128:(h + 1) * 128], kftok[:, h * 128:(h + 1) * 128],
                         vtok[:, h * 128:(h + 1) * 128], start=False, stop=(c == NCH - 1),
                         skip_group_check=True)
            _mm.then_inc(tn.h, 1)
        tn.bump()

        @VE
        def _(v, c=c, tgt=gp.n, b=c % 2):
            v.wait_ge(gp.h, tgt)
            for h in range(H):
                v.reduce_sum(kredc[:, h:h + 1], r32(kft[b][:, h * 128:(h + 1) * 128]), axis=AXX)
            v.engine_nop()
            v.engine_nop()
            v.engine_nop()
            v.engine_nop()
            v.tensor_add(ktot[:, ((c + 1) % 2) * H:((c + 1) % 2 + 1) * H],
                         ktot[:, (c % 2) * H:(c % 2 + 1) * H], kredc[:])
            v.tensor_copy(vspace[:], zeros8[:]).then_inc(vc.h, 1)
        vc.bump()

    if _stage_on('CC'):
        @VE
        def _(v, tgt=tn.n):
            v.wait_ge(tn.h, tgt)
            v.tensor_copy(stot, PB[:])
            v.tensor_copy(vspace[:], zeros8[:]).then_inc(vc.h, 1)
        vc.bump()

        @GE
        def _(g, tgt=vc.n, base=gd.n):
            g.wait_ge(vc.h, tgt)
            g.dma_start(out=cc_in[0:1, 0:H * 128 * 128].rearrange("o (p n) -> (o p) n", p=128),
                        in_=r32(stot)).then_inc(gd.h, 16)
            g.dma_start(out=cc_in[0:1, H * 128 * 128:].rearrange("o (p n) -> (o p) n", p=128),
                        in_=ktot[:, (NCH % 2) * H:(NCH % 2 + 1) * H]).then_inc(gd.h, 16)
            g.wait_ge(gd.h, base + 32)
            g.collective_compute(
                "AllGather", ALU.bypass,
                replica_groups=[list(range(NCORE))],
                ins=[cc_in.opt()],
                outs=[cc_out.opt()],
            ).then_inc(gp.h, 1)
        gd.bump(32)
        gp.bump()

        @VE
        def _(v):
            for j in range(4):
                v.tensor_copy(Sacc[0][:, j * 512:(j + 1) * 512], zerosT[:])
            v.tensor_copy(Sacc[0][:, 4 * 512:4 * 512 + H], zerosT[:, 0:H])
            v.tensor_copy(vspace[:], zeros8[:]).then_inc(vc.h, 1)
        vc.bump()

        for cp in range(NCORE):
            @GE
            def _(g, cp=cp, tgt=gp.n, tv=vc.n, b=cp % 2):
                g.wait_ge(gp.h, tgt)
                g.wait_ge(vc.h, tv)
                g.dma_start(out=ccld[b][:, 0:D],
                            in_=rr(cc_out[cp:cp + 1, 0:H * 128 * 128]).rearrange("o (p n) -> (o p) n", p=128)).then_inc(gd.h, 16)
                g.dma_start(out=ccld[b][:, D:D + H],
                            in_=rr(cc_out[cp:cp + 1, H * 128 * 128:]).rearrange("o (p n) -> (o p) n", p=128)).then_inc(gd.h, 16)
            gd.bump(32)

            @VE
            def _(v, cp=cp, tgt=gd.n, b=cp % 2):
                v.wait_ge(gd.h, tgt)
                v.scalar_tensor_tensor(out=Sacc[0], in0=r32(ccld[b]), scalar=prefm[:, cp:cp + 1],
                                       in1=r32(Sacc[0]), op0=ALU.mult, op1=ALU.add)
                v.tensor_copy(vspace[:], zeros8[:]).then_inc(vc.h, 1)
            vc.bump()


    if DEBUG:
        @GE
        def _(g, tgt=vc.n):
            g.wait_ge(vc.h, tgt)
            g.dma_start(out=stdbg_s[0, :, 0:D], in_=r32(stot)).then_inc(gd.h, 16)
            g.dma_start(out=stdbg_s[0, :, D:D + H], in_=ktot[:, (NCH % 2) * H:(NCH % 2 + 1) * H]).then_inc(gd.h, 16)
            g.dma_start(out=stdbg_s[1], in_=r32(Sacc[0])).then_inc(gd.h, 16)
        gd.bump(48)

    for c in range(NCH if _stage_on('C2') else 0):
        cur, nxt = Sacc[c % 2], Sacc[(c + 1) % 2]
        Srun, ksrun = cur[:, 0:D], cur[:, D:D + H]
        Snxt, ksnxt = nxt[:, 0:D], nxt[:, D:D + H]

        load_chunk_qkv(c, want_q=True)
        transpose_to_tok(kft[c % 2], kftok)
        transpose_to_tok(vtt[c % 2], vtok)

        for h in range(H):
            at_sl = PA[:, 0:128] if h % 2 == 0 else PA[:, 512:640]
            io_sl = PA[:, 1024:1152]
            z_sl = PA[0:1, 1536:1664]
            zb_sl = PA[:, 1664:1792]
            kv_sl = PB[:, 0:128] if h % 2 == 0 else PB[:, 512:640]
            qsl = qft[c % 2][:, h * 128:(h + 1) * 128]
            kfsl = kft[c % 2][:, h * 128:(h + 1) * 128]
            ktsl = kftok[:, h * 128:(h + 1) * 128]
            vtsl = vtok[:, h * 128:(h + 1) * 128]
            ab = atm[h % 2]

            @TE
            def _(t, at_sl=at_sl, kfsl=kfsl, qsl=qsl, tgt=vc.n):
                t.wait_ge(vc.h, tgt)
                _mm = t.matmul(at_sl, kfsl, qsl, start=True, stop=True)
                _mm.then_inc(tn.h, 1)
            tn.bump()

            @VE
            def _(v, at_sl=at_sl, ab=ab, tgt=tn.n):
                v.wait_ge(tn.h, tgt)
                v.tensor_tensor(ab, at_sl, maskut[:], op=ALU.mult)
                v.tensor_copy(vspace[:], zeros8[:]).then_inc(vc.h, 1)
            vc.bump()

            @TE
            def _(t, io_sl=io_sl, z_sl=z_sl, kv_sl=kv_sl, h=h, ab=ab,
                  qsl=qsl, vtsl=vtsl, ktsl=ktsl, Srun=Srun, ksrun=ksrun, tgt=vc.n):
                t.wait_ge(vc.h, tgt)
                t.matmul(io_sl, Srun[:, h * 128:(h + 1) * 128], qsl, start=True, stop=False)
                t.matmul(io_sl, vtsl, ab, start=False, stop=True)
                t.matmul(z_sl, ksrun[:, h:h + 1], qsl, start=True, stop=False)
                t.matmul(z_sl, onecol[:], ab, start=False, stop=True)
                _mm = t.matmul(kv_sl, ktsl, vtsl, start=True, stop=True)
                _mm.then_inc(tn.h, 1)
            tn.bump()

            @VE
            def _(v, z_sl=z_sl, h=h, tgt=tn.n):
                v.wait_ge(tn.h, tgt)
                v.tensor_scalar(out=zrow[:], in0=z_sl, scalar1=EPS_Z, scalar2=None, op0=ALU.add)
                v.engine_nop()
                v.engine_nop()
                v.engine_nop()
                v.engine_nop()
                v.reciprocal(zall[0:1, :], zrow[:])
                v.tensor_copy(vspace[:], zeros8[:]).then_inc(vc.h, 1)
            vc.bump()

            @TE
            def _(t, zb_sl=zb_sl, h=h, tgt=vc.n):
                t.wait_ge(vc.h, tgt)
                _mm = t.matmul(zb_sl, selz[:, 0:128], zall[:], start=True, stop=True)
                _mm.then_inc(tn.h, 1)
            tn.bump()

            @VE
            def _(v, zb_sl=zb_sl, io_sl=io_sl, kv_sl=kv_sl, h=h, kfsl=kfsl,
                  Srun=Srun, Snxt=Snxt, ksrun=ksrun, ksnxt=ksnxt, tgt=tn.n):
                v.wait_ge(tn.h, tgt)
                v.tensor_copy(zbsb, zb_sl)
                v.engine_nop()
                v.engine_nop()
                v.engine_nop()
                v.engine_nop()
                v.tensor_tensor(aoTsb[:, h * 128:(h + 1) * 128], io_sl, r32(zbsb), op=ALU.mult)
                v.tensor_add(Snxt[:, h * 128:(h + 1) * 128], r32(Srun[:, h * 128:(h + 1) * 128]), kv_sl)
                v.reduce_sum(kredc[:, h:h + 1], r32(kfsl), axis=AXX)
                v.engine_nop()
                v.engine_nop()
                v.engine_nop()
                v.engine_nop()
                v.tensor_add(ksnxt[:, h:h + 1], r32(ksrun[:, h:h + 1]), kredc[:, h:h + 1])
                v.tensor_copy(vspace[:], zeros8[:]).then_inc(vc.h, 1)
            vc.bump()

        @GE
        def _(g, c=c, tgt=vc.n):
            g.wait_ge(vc.h, tgt)
            g.dma_start(out=aoT_s[c].rearrange("h p f -> p h f"),
                        in_=aoTsb.rearrange("p (h f) -> p h f", h=DC)).then_inc(gd.h, 16)
        gd.bump(16)

    A.reset()
    aot = A.take(DC * TS)
    ybufT = A.take(DC * TS)
    wstD = [A.take(D) for _ in range(2)]
    ytok = A.take(D)
    lnY = A.take(D)
    x2b = A.take(D)
    xln2 = A.take(D)
    sqD = A.take(D)
    xstage = A.take(D)
    xl2stg = A.take(D)
    aot_v = aot.rearrange("p (h t) -> p h t", h=DC)
    assert A.off <= ARENA_N - UE_N

    for ts in range(NTS if _stage_on('D') else 0):
        @GE
        def _(g, ts=ts, tgt=tn.n, tv=vc.n):
            g.wait_ge(tn.h, tgt)
            g.wait_ge(vc.h, tv)
            for fch in range(DC):
                g.dma_start(out=aot_v[:, fch, :].rearrange("p (c f) -> p c f", c=4),
                            in_=aoT_s[ts * 4:(ts + 1) * 4, fch].rearrange("c p f -> p c f")).then_inc(gd.h, 16)
        gd.bump(16 * DC)

        wd_use = [tn.n, tn.n]
        for jc in range(DC):
            @SY
            def _(e, jc=jc, tgt=wd_use[jc % 2]):
                e.wait_ge(tn.h, tgt)
                e.dma_start(out=wstD[jc % 2].rearrange("p (k f) -> p k f", k=DC),
                            in_=rr(wo_b[jc]).rearrange("k p f -> p k f")).then_inc(wd.h, 16)
            wd.bump(16)

            pa_sl = PA[:, (jc % 2) * 512:(jc % 2) * 512 + 512]

            @TE
            def _(t, jc=jc, pa_sl=pa_sl, tgt=wd.n, tg=gd.n, tsc=sc.n):
                t.wait_ge(wd.h, tgt)
                t.wait_ge(gd.h, tg)
                t.wait_ge(sc.h, tsc)
                for k in range(DC):
                    _mm = t.matmul(pa_sl, wstD[jc % 2][:, k * 128:(k + 1) * 128], aot_v[:, k, :],
                             start=(k == 0), stop=(k == DC - 1))
                _mm.then_inc(tn.h, 1)
            tn.bump()
            wd_use[jc % 2] = tn.n

            @SE
            def _(s, jc=jc, pa_sl=pa_sl, tgt=tn.n):
                s.wait_ge(tn.h, tgt)
                s.activation(out=ybufT[:, jc * 512:(jc + 1) * 512], in_=pa_sl, func=AF.Identity,
                             bias=bo_t[:, jc:jc + 1])
                s.activation(out=sspace[:], in_=zeros8[:], func=AF.Identity).then_inc(sc.h, 1)
            sc.bump()

        for tc in range(4):
            gtok = ts * 4 + tc

            @TE
            def _(t, tc=tc, tgt=sc.n, tv=vc.n):
                t.wait_ge(sc.h, tgt)
                t.wait_ge(vc.h, tv)
                for jc in range(DC):
                    _mm = t.transpose(rr(PB[:, jc * 128:(jc + 1) * 128]),
                                ybufT[:, jc * 512 + tc * 128: jc * 512 + (tc + 1) * 128], identr[:])
                _mm.then_inc(tn.h, 1)
            tn.bump()

            @VE
            def _(v, tgt=tn.n):
                v.wait_ge(tn.h, tgt)
                v.tensor_copy(ytok, rr(PB[:]))
                v.tensor_copy(vspace[:], zeros8[:]).then_inc(vc.h, 1)
            vc.bump()

            @SY
            def _(e, gtok=gtok, tgt=vc.n):
                e.wait_ge(vc.h, tgt)
                e.dma_start(out=xstage, in_=rr(x_in[gtok * 128:(gtok + 1) * 128, :])).then_inc(wd.h, 16)
            wd.bump(16)

            ln_stats(r32(ytok), D, sqD)
            ln_apply(r32(ytok), lnY)

            @VE
            def _(v, tgt=sc.n, tw=wd.n):
                v.wait_ge(sc.h, tgt)
                v.wait_ge(wd.h, tw)
                v.tensor_add(x2b, r32(xstage), r32(lnY))
                v.tensor_copy(vspace[:], zeros8[:]).then_inc(vc.h, 1)
            vc.bump()

            @GE
            def _(g, gtok=gtok, tgt=vc.n):
                g.wait_ge(vc.h, tgt)
                g.dma_start(out=x2_s[gtok * 128:(gtok + 1) * 128, :], in_=r32(x2b)).then_inc(gd.h, 16)
            gd.bump(16)

            ln_stats(r32(x2b), D, sqD)
            ln_apply(r32(x2b), xln2)

            @SE
            def _(s):
                _ls = s.activation(out=sqD, in_=r32(x2b), func=AF.Square)
                _ls
                s.activation(out=sspace[:], in_=zeros8[:], func=AF.Identity).then_inc(sc.h, 1)
            sc.bump()

            if DEBUG:
                @GE
                def _(g, gtok=gtok, tgt=sc.n):
                    g.wait_ge(sc.h, tgt)
                    g.dma_start(out=xln2dbg_s[gtok], in_=xln2).then_inc(gd.h, 16)
                gd.bump(16)

            @TE
            def _(t, tgt=sc.n, tv=vc.n):
                t.wait_ge(sc.h, tgt)
                t.wait_ge(vc.h, tv)
                for k in range(DC):
                    _mm = t.transpose(rr(PB[:, k * 128:(k + 1) * 128]), xln2[:, k * 128:(k + 1) * 128], identr[:])
                _mm.then_inc(tn.h, 1)
            tn.bump()

            if DEBUG:
                @VE
                def _(v, tgt=tn.n):
                    v.wait_ge(tn.h, tgt)
                    v.tensor_copy(sqD, rr(PB[:]))
                    v.tensor_copy(vspace[:], zeros8[:]).then_inc(vc.h, 1)
                vc.bump()

                @GE
                def _(g, gtok=gtok, tgt=vc.n):
                    g.wait_ge(vc.h, tgt)
                    g.dma_start(out=xl2dbg2_s[gtok], in_=sqD).then_inc(gd.h, 16)
                gd.bump(16)

            @VE
            def _(v, tgt=tn.n, tg=gd.n):
                v.wait_ge(tn.h, tgt)
                v.wait_ge(gd.h, tg)
                v.tensor_copy(xl2stg, rr(PB[:]))
                v.tensor_copy(vspace[:], zeros8[:]).then_inc(vc.h, 1)
            vc.bump()

            @GE
            def _(g, gtok=gtok, tgt=vc.n):
                g.wait_ge(vc.h, tgt)
                g.dma_start(out=xl2T_s[:, :, gtok * 128:(gtok + 1) * 128].rearrange("d p t -> p d t"),
                            in_=xl2stg.rearrange("p (d t) -> p d t", d=DC)).then_inc(gd.h, 16)
            gd.bump(16)

    A.reset()
    xl2t = A.take(DC * TS)
    a1t = A.take(D)
    a3t = A.take(D)
    w1s = [A.take(D) for _ in range(2)]
    w3s = [A.take(D) for _ in range(2)]
    b1s = [A.take(E * 128, rows=R) for _ in range(2)]
    b3s = [A.take(E * 128, rows=R) for _ in range(2)]
    a2s = [A.take(128) for _ in range(2)]
    base1sb = A.take(TS)
    base3sb = A.take(TS)
    ssb = A.take(TS)
    pwsb = A.take(TS)
    pwwsb = [A.take(TS) for _ in range(2)]
    Wb = A.take(E * TS)
    t1c = [A.take(TS, rows=R) for _ in range(E)]
    t3c = [A.take(TS, rows=R) for _ in range(E)]
    lgE = A.take(TS, rows=E)
    w8b = A.take(E)
    w8T = A.take(TS, rows=E)
    waccsb = [A.take(TS) for _ in range(2)]
    usb = A.take(TS)
    assert A.off <= ARENA_N - UE_N, A.off
    xl2t_v = xl2t.rearrange("p (d t) -> p d t", d=DC)

    A.reset()
    w2sF = A.take(FC * 128)
    b2sF = A.take(E * 128, rows=R)
    waccR = A.take(FC * TS)
    o2ld = A.take(D)
    o2tok = A.take(D)
    lnO = A.take(D)
    x2bF = A.take(D)
    sqF = A.take(D)
    o2st = A.take(TS)
    assert A.off <= ARENA_N - UE_N, A.off
    w2sF_v = w2sF.rearrange("p (k f) -> p k f", k=FC)
    waccR_v = waccR.rearrange("p (k t) -> p k t", k=FC)

    for ts in range(NTS if _stage_on('EF') else 0):
        @GE
        def _(g, ts=ts, tgt=tn.n, tv=vc.n, tsc=sc.n):
            g.wait_ge(tn.h, tgt)
            g.wait_ge(vc.h, tv)
            g.wait_ge(sc.h, tsc)
            for dd in range(DC):
                g.dma_start(out=xl2t_v[:, dd, :],
                            in_=xl2T_s[dd, :, ts * TS:(ts + 1) * TS]).then_inc(gd.h, 16)
        gd.bump(16 * DC)

        @SY
        def _(e, tgt=tn.n):
            e.wait_ge(tn.h, tgt)
            e.dma_start(out=a1t.rearrange("p (k f) -> p k f", k=DC),
                        in_=rr(a1T_b).rearrange("k p f -> p k f")).then_inc(wd.h, 16)
            e.dma_start(out=a3t.rearrange("p (k f) -> p k f", k=DC),
                        in_=rr(a3T_b).rearrange("k p f -> p k f")).then_inc(wd.h, 16)
        wd.bump(32)

        if EF_SUB >= 2:
            @TE
            def _(t, tgt=gd.n, tw=wd.n, tv=vc.n):
                t.wait_ge(gd.h, tgt)
                t.wait_ge(wd.h, tw)
                t.wait_ge(vc.h, tv)
                for k in range(DC):
                    _mm = t.matmul(PB[0:E, 1536:1536 + TS], gw_t[:, k * E:(k + 1) * E], xl2t_v[:, k, :],
                             start=(k == 0), stop=(k == DC - 1))
                _mm.then_inc(tn.h, 1)
            tn.bump()

            @VE
            def _(v, tgt=tn.n):
                v.wait_ge(tn.h, tgt)
                v.tensor_copy(lgE, rr(PB[0:E, 1536:1536 + TS]))
                v.tensor_copy(vspace[:], zeros8[:]).then_inc(vc.h, 1)
            vc.bump()

            for tc in range(4):
                @TE
                def _(t, tc=tc, tgt=vc.n):
                    t.wait_ge(vc.h, tgt)
                    _mm = t.transpose(rr(PB[:, 1024:1024 + E]), lgE[:, tc * 128:(tc + 1) * 128], identr[0:E, 0:E])
                    _mm.then_inc(tn.h, 1)
                tn.bump()

                @VE
                def _(v, tgt=tn.n):
                    v.wait_ge(tn.h, tgt)
                    v.tensor_copy(lgt[:], PB[:, 1024:1024 + E])
                    v.engine_nop()
                    v.engine_nop()
                    v.engine_nop()
                    v.engine_nop()
                    v.reduce_max(tp2[:, 0:1], lgt[:], axis=AXX, negate=True)
                    v.engine_nop()
                    v.engine_nop()
                    v.engine_nop()
                    v.engine_nop()
                    v.scalar_tensor_tensor(out=tp8[:, 0:8], in0=lgt[:], scalar=tp2[:, 0:1],
                                           in1=zeros8[:], op0=ALU.add, op1=ALU.add)
                    v.engine_nop()
                    v.engine_nop()
                    v.engine_nop()
                    v.engine_nop()
                    v.tensor_scalar(out=tpeq[:, 0:8], in0=tp8[:, 0:8], scalar1=0.0, scalar2=None,
                                    op0=ALU.is_equal)
                    v.engine_nop()
                    v.engine_nop()
                    v.engine_nop()
                    v.engine_nop()
                    v.tensor_copy(tp8[:, 8:16], tpeq[:, 0:8])
                    v.engine_nop()
                    v.engine_nop()
                    v.engine_nop()
                    v.engine_nop()
                    v.scalar_tensor_tensor(out=tp8[:, 16:24], in0=tp8[:, 8:16], scalar=-1e30,
                                           in1=lgt[:], op0=ALU.mult, op1=ALU.add)
                    v.engine_nop()
                    v.engine_nop()
                    v.engine_nop()
                    v.engine_nop()
                    v.reduce_max(tp2[:, 1:2], tp8[:, 16:24], axis=AXX, negate=True)
                    v.engine_nop()
                    v.engine_nop()
                    v.engine_nop()
                    v.engine_nop()
                    v.scalar_tensor_tensor(out=tp8[:, 24:32], in0=tp8[:, 16:24], scalar=tp2[:, 1:2],
                                           in1=zeros8[:], op0=ALU.add, op1=ALU.add)
                    v.engine_nop()
                    v.engine_nop()
                    v.engine_nop()
                    v.engine_nop()
                    v.tensor_scalar(out=tpeq[:, 8:16], in0=tp8[:, 24:32], scalar1=0.0, scalar2=None,
                                    op0=ALU.is_equal)
                    v.engine_nop()
                    v.engine_nop()
                    v.engine_nop()
                    v.engine_nop()
                    v.tensor_copy(tp8[:, 32:40], tpeq[:, 8:16])
                    v.tensor_sub(tp2[:, 2:3], tp2[:, 1:2], tp2[:, 0:1])
                    v.tensor_copy(vspace[:], zeros8[:]).then_inc(vc.h, 1)
                vc.bump()

                @SE
                def _(s, tgt=vc.n):
                    s.wait_ge(vc.h, tgt)
                    s.activation(out=tp2[:, 3:4], in_=tp2[:, 2:3], func=AF.Sigmoid)
                    s.activation(out=sspace[:], in_=zeros8[:], func=AF.Identity).then_inc(sc.h, 1)
                sc.bump()

                @VE
                def _(v, tgt=sc.n):
                    v.wait_ge(sc.h, tgt)
                    v.tensor_scalar(out=tp2[:, 4:5], in0=tp2[:, 3:4], scalar1=-1.0, scalar2=1.0,
                                    op0=ALU.mult, op1=ALU.add)
                    v.engine_nop()
                    v.engine_nop()
                    v.engine_nop()
                    v.engine_nop()
                    v.scalar_tensor_tensor(out=tp8[:, 0:8], in0=tp8[:, 8:16], scalar=tp2[:, 3:4],
                                           in1=zeros8[:], op0=ALU.mult, op1=ALU.add)
                    v.engine_nop()
                    v.engine_nop()
                    v.engine_nop()
                    v.engine_nop()
                    v.scalar_tensor_tensor(out=w8b, in0=tp8[:, 32:40], scalar=tp2[:, 4:5],
                                           in1=tp8[:, 0:8], op0=ALU.mult, op1=ALU.add)
                    v.tensor_copy(vspace[:], zeros8[:]).then_inc(vc.h, 1)
                vc.bump()

                @TE
                def _(t, tc=tc, tgt=vc.n):
                    t.wait_ge(vc.h, tgt)
                    _mm = t.transpose(rr(PB[0:E, 1280:1280 + 128]), w8b, identr[:])
                    _mm.then_inc(tn.h, 1)
                tn.bump()

                @VE
                def _(v, tc=tc, tgt=tn.n):
                    v.wait_ge(tn.h, tgt)
                    v.tensor_copy(w8T[:, tc * 128:(tc + 1) * 128], rr(PB[0:E, 1280:1280 + 128]))
                    v.tensor_copy(vspace[:], zeros8[:]).then_inc(vc.h, 1)
                vc.bump()

            for e_ in range(E):
                @TE
                def _(t, e_=e_, tgt=vc.n):
                    t.wait_ge(vc.h, tgt)
                    _mm = t.matmul(PB[:, 1536:1536 + TS], sel8[:, e_ * 128:(e_ + 1) * 128], w8T[:],
                             start=True, stop=True)
                    _mm.then_inc(tn.h, 1)
                tn.bump()

                @VE
                def _(v, e_=e_, tgt=tn.n):
                    v.wait_ge(tn.h, tgt)
                    v.tensor_copy(Wb[:, e_ * TS:(e_ + 1) * TS], PB[:, 1536:1536 + TS])
                    v.tensor_copy(vspace[:], zeros8[:]).then_inc(vc.h, 1)
                vc.bump()


        if DEBUG and ts == 0:
            @GE
            def _(g, tgt=vc.n):
                g.wait_ge(vc.h, tgt)
                g.dma_start(out=wbdbg_s[:], in_=rr(Wb)).then_inc(gd.h, 16)
                g.dma_start(out=ffdbg_s[6, 0:E, :], in_=w8T).then_inc(gd.h, 16)
                g.dma_start(out=ffdbg_s[7, 0:E, :], in_=lgE).then_inc(gd.h, 16)
                g.dma_start(out=ffdbg_s[5, :, 0:40], in_=rr(tp8[:])).then_inc(gd.h, 16)
                g.dma_start(out=ffdbg_s[5, :, 64:74], in_=rr(tp2[:])).then_inc(gd.h, 16)
                g.dma_start(out=ffdbg_s[5, :, 80:88], in_=rr(w8b)).then_inc(gd.h, 16)
            gd.bump(96)

        if EF_SUB >= 3:
            for e_ in range(min(E, T13_N)):
                for (at, tcb) in ((a1t, t1c[e_]), (a3t, t3c[e_])):
                    @TE
                    def _(t, at=at, e_=e_, tgt=vc.n):
                        t.wait_ge(vc.h, tgt)
                        for k in range(DC):
                            _mm = t.matmul(PB[0:R, 1536:1536 + TS],
                                     at[:, k * 128 + e_ * R:k * 128 + (e_ + 1) * R],
                                     xl2t_v[:, k, :], start=(k == 0), stop=(k == DC - 1))
                        _mm.then_inc(tn.h, 1)
                    tn.bump()

                    @VE
                    def _(v, tcb=tcb, tgt=tn.n):
                        v.wait_ge(tn.h, tgt)
                        v.tensor_scalar(out=tcb, in0=PB[0:R, 1536:1536 + TS], scalar1=2.0, scalar2=None,
                                        op0=ALU.mult)
                        v.tensor_copy(vspace[:], zeros8[:]).then_inc(vc.h, 1)
                    vc.bump()

        @TE
        def _(t, tgt=vc.n):
            t.wait_ge(vc.h, tgt)
            _mm = t.matmul(PB[:, 512:528], ident32[:], zerosT[:, 0:16], start=True, stop=True,
                           skip_group_check=True)
            _mm.then_inc(tn.h, 1)
        tn.bump()

        e_use = [tn.n, tn.n]
        for fc in range(FC if EF_SUB >= 4 else 0):
            b = fc % 2

            @SY
            def _(e, fc=fc, b=b, tgt=e_use[b]):
                e.wait_ge(tn.h, tgt)
                e.dma_start(out=w1s[b].rearrange("p (k f) -> p k f", k=DC),
                            in_=rr(w1T_b[fc]).rearrange("(k p) f -> p k f", p=128)).then_inc(wd.h, 16)
                e.dma_start(out=w3s[b].rearrange("p (k f) -> p k f", k=DC),
                            in_=rr(w3T_b[fc]).rearrange("(k p) f -> p k f", p=128)).then_inc(wd.h, 16)
                e.dma_start(out=b1s[b].rearrange("r (e f) -> r e f", e=E),
                            in_=rr(b1T_b[fc]).rearrange("e r f -> r e f")).then_inc(wd.h, 16)
                e.dma_start(out=b3s[b].rearrange("r (e f) -> r e f", e=E),
                            in_=rr(b3T_b[fc]).rearrange("e r f -> r e f")).then_inc(wd.h, 16)
                e.dma_start(out=a2s[b], in_=rr(a2T_b[fc])).then_inc(wd.h, 16)
            wd.bump(80)

            @TE
            def _(t, fc=fc, b=b, tgt=wd.n, tv=vc.n):
                t.wait_ge(wd.h, tgt)
                t.wait_ge(vc.h, tv)
                for k in range(DC):
                    t.matmul(PA[:, 0:512], w1s[b][:, k * 128:(k + 1) * 128], xl2t_v[:, k, :],
                             start=(k == 0), stop=(k == DC - 1))
                for k in range(DC):
                    _mm = t.matmul(PA[:, 512:1024], w3s[b][:, k * 128:(k + 1) * 128], xl2t_v[:, k, :],
                             start=(k == 0), stop=(k == DC - 1))
                _mm.then_inc(tn.h, 1)
            tn.bump()

            @VE
            def _(v, tgt=tn.n):
                v.wait_ge(tn.h, tgt)
                v.tensor_copy(base1sb, rr(PA[:, 0:512]))
                v.tensor_copy(base3sb, rr(PA[:, 512:1024]))
                v.tensor_copy(vspace[:], zeros8[:]).then_inc(vc.h, 1)
            vc.bump()

            h_sc = [sc.n, sc.n]
            h_vc = [vc.n, vc.n]

            for e_ in range(E):
                h1bk = PA[:, 1024:1536] if e_ % 2 == 0 else PB[:, 1024:1536]
                h3bk = PA[:, 1536:2048] if e_ % 2 == 0 else PB[:, 1536:2048]

                @TE
                def _(t, b=b, e_=e_, h1bk=h1bk, tgt=h_vc[e_ % 2], tsc=h_sc[e_ % 2]):
                    t.wait_ge(vc.h, tgt)
                    t.wait_ge(sc.h, tsc)
                    t.matmul(h1bk, identr[:], base1sb, start=True, stop=False,
                             skip_group_check=True)
                    _mm = t.matmul(h1bk, b1s[b][:, e_ * 128:(e_ + 1) * 128], t1c[e_],
                             start=False, stop=True, skip_group_check=True)
                    _mm.then_inc(tn.h, 1)
                tn.bump()

                @SE
                def _(s, h1bk=h1bk, tgt=tn.n):
                    s.wait_ge(tn.h, tgt)
                    s.activation(out=ssb, in_=h1bk, func=AF.Silu)
                    s.activation(out=sspace[:], in_=zeros8[:], func=AF.Identity).then_inc(sc.h, 1)
                sc.bump()

                @TE
                def _(t, fc=fc, b=b, e_=e_, h3bk=h3bk, tgt=h_vc[e_ % 2], tprev=vc.n):
                    t.wait_ge(vc.h, tgt)
                    _mm = t.matmul(h3bk, identr[:], base3sb, start=True, stop=False,
                                   skip_group_check=True)
                    _mm = t.matmul(h3bk, b3s[b][:, e_ * 128:(e_ + 1) * 128], t3c[e_],
                                   start=False, stop=True, skip_group_check=True)
                    if e_ > 0:
                        t.wait_ge(vc.h, tprev)
                        pbp = pwwsb[(e_ - 1) % 2]
                        _mm = t.matmul(PB[:, 0:512], identr[:], pbp, start=(e_ - 1 == 0), stop=False,
                                       skip_group_check=True)
                        for tc in range(4):
                            sl = ((e_ - 1) * 4 + tc) * R
                            _mm = t.matmul(PB[:, 512 + sl:512 + sl + R],
                                           pbp[:, tc * 128:(tc + 1) * 128], a2s[b][:, (e_ - 1) * R:e_ * R],
                                           start=False, stop=(fc == FC - 1), skip_group_check=True)
                    _mm.then_inc(tn.h, 1)
                tn.bump()

                @VE
                def _(v, e_=e_, h3bk=h3bk, tgt=tn.n, tsc=sc.n, pb=pwwsb[e_ % 2]):
                    v.wait_ge(tn.h, tgt)
                    v.wait_ge(sc.h, tsc)
                    v.tensor_tensor(pwsb, r32(ssb), h3bk, op=ALU.mult)
                    v.tensor_tensor(pb, r32(pwsb), r32(Wb[:, e_ * TS:(e_ + 1) * TS]), op=ALU.mult)
                    v.tensor_copy(vspace[:], zeros8[:]).then_inc(vc.h, 1)
                vc.bump()
                h_sc[e_ % 2] = sc.n
                h_vc[e_ % 2] = vc.n

            @TE
            def _(t, fc=fc, b=b, tgt=vc.n, pb=pwwsb[(E - 1) % 2]):
                t.wait_ge(vc.h, tgt)
                t.matmul(PB[:, 0:512], identr[:], pb, start=False, stop=True,
                         skip_group_check=True)
                for tc in range(4):
                    sl = ((E - 1) * 4 + tc) * R
                    _mm = t.matmul(PB[:, 512 + sl:512 + sl + R],
                             pb[:, tc * 128:(tc + 1) * 128], a2s[b][:, (E - 1) * R:E * R],
                             start=False, stop=(fc == FC - 1), skip_group_check=True)
                _mm.then_inc(tn.h, 1)
            tn.bump()

            if DEBUG and ts == 0 and fc == 0:
                @GE
                def _(g, tgt=vc.n, tsc=sc.n):
                    g.wait_ge(vc.h, tgt)
                    g.wait_ge(sc.h, tsc)
                    g.dma_start(out=ffdbg_s[0], in_=base1sb).then_inc(gd.h, 16)
                    g.dma_start(out=ffdbg_s[1], in_=base3sb).then_inc(gd.h, 16)
                    g.dma_start(out=ffdbg_s[2], in_=ssb).then_inc(gd.h, 16)
                    g.dma_start(out=ffdbg_s[3], in_=pwsb).then_inc(gd.h, 16)
                    g.dma_start(out=ffdbg_s[4], in_=pwwsb[(E - 1) % 2]).then_inc(gd.h, 16)
                    g.dma_start(out=ffdbg_s[5, 0:R, :], in_=t1c[0]).then_inc(gd.h, 16)
                gd.bump(96)

            e_use[b] = tn.n

            @VE
            def _(v, fc=fc, wb=waccsb[fc % 2], tgt=tn.n, tg=gd.n):
                v.wait_ge(tn.h, tgt)
                v.wait_ge(gd.h, tg)
                v.tensor_copy(wb, rr(PB[:, 0:512]))
                v.tensor_copy(vspace[:], zeros8[:]).then_inc(vc.h, 1)
            vc.bump()

            @GE
            def _(g, ts=ts, fc=fc, wb=waccsb[fc % 2], tgt=vc.n):
                g.wait_ge(vc.h, tgt)
                g.dma_start(out=wacc_s[ts, fc], in_=wb).then_inc(gd.h, 16)
            gd.bump(16)

        if EF_SUB >= 5:
            @VE
            def _(v, tgt=tn.n):
                v.wait_ge(tn.h, tgt)
                v.tensor_copy(usb, rr(PB[:, 512:1024]))
                v.tensor_copy(vspace[:], zeros8[:]).then_inc(vc.h, 1)
            vc.bump()

            for e_ in range(E):
                for tc in range(4):
                    slot = 1024 + ((e_ * 4 + tc) % 8) * 128

                    @TE
                    def _(t, e_=e_, tc=tc, slot=slot, tgt=vc.n):
                        t.wait_ge(vc.h, tgt)
                        _mm = t.transpose(rr(PB[0:R, slot:slot + 128]),
                                    usb[:, (e_ * 4 + tc) * R:(e_ * 4 + tc + 1) * R], identr[:])
                        _mm.then_inc(tn.h, 1)
                    tn.bump()

                    @VE
                    def _(v, e_=e_, tc=tc, slot=slot, tgt=tn.n):
                        v.wait_ge(tn.h, tgt)
                        v.tensor_scalar(out=ueT[:, e_ * TS + tc * 128:e_ * TS + (tc + 1) * 128],
                                        in0=rr(PB[0:R, slot:slot + 128]),
                                        scalar1=2.0, scalar2=None, op0=ALU.mult)
                        v.tensor_copy(vspace[:], zeros8[:]).then_inc(vc.h, 1)
                    vc.bump()


        if DEBUG and ts == 0:
            @GE
            def _(g, tgt=vc.n):
                g.wait_ge(vc.h, tgt)
                g.dma_start(out=wbdbg_s[0:R, :], in_=ueT).then_inc(gd.h, 16)
            gd.bump(16)

        if EF_SUB >= 6:
            @GE
            def _(g, ts=ts, tgt=vc.n, tsc=sc.n):
                g.wait_ge(vc.h, tgt)
                g.wait_ge(sc.h, tsc)
                g.dma_start(out=waccR_v, in_=wacc_s[ts].rearrange("k p t -> p k t")).then_inc(gd.h, 16)
            gd.bump(16)

            for dc in range(DC):
                @SY
                def _(e, dc=dc, tgt=tn.n):
                    e.wait_ge(tn.h, tgt)
                    e.dma_start(out=w2sF_v, in_=rr(w2T_b[dc]).rearrange("k p f -> p k f")).then_inc(wd.h, 16)
                    e.dma_start(out=b2sF.rearrange("r (e f) -> r e f", e=E),
                                in_=rr(b2T_b[dc]).rearrange("e r f -> r e f")).then_inc(wd.h, 16)
                wd.bump(32)

                pa_sl = PA[:, (dc % 2) * 512:(dc % 2) * 512 + 512]

                @TE
                def _(t, dc=dc, pa_sl=pa_sl, tgt=wd.n, tg=gd.n, tv=vc.n):
                    t.wait_ge(wd.h, tgt)
                    t.wait_ge(gd.h, tg)
                    t.wait_ge(vc.h, tv)
                    for k in range(FC):
                        t.matmul(pa_sl, w2sF_v[:, k, :], waccR_v[:, k, :],
                                 start=(k == 0), stop=False, skip_group_check=True)
                    for e_ in range(E):
                        _mm = t.matmul(pa_sl, b2sF[:, e_ * 128:(e_ + 1) * 128], ueT[:, e_ * TS:(e_ + 1) * TS],
                                 start=False, stop=(e_ == E - 1), skip_group_check=True)
                    _mm.then_inc(tn.h, 1)
                tn.bump()

                @VE
                def _(v, pa_sl=pa_sl, tgt=tn.n, tg=gd.n):
                    v.wait_ge(tn.h, tgt)
                    v.wait_ge(gd.h, tg)
                    v.tensor_copy(o2st, rr(pa_sl))
                    v.tensor_copy(vspace[:], zeros8[:]).then_inc(vc.h, 1)
                vc.bump()

                @GE
                def _(g, ts=ts, dc=dc, tgt=vc.n):
                    g.wait_ge(vc.h, tgt)
                    g.dma_start(out=out2_s[ts, dc], in_=o2st).then_inc(gd.h, 16)
                gd.bump(16)


        for tc in range(4 if EF_SUB >= 7 else 0):
            gtok = ts * 4 + tc

            @GE
            def _(g, ts=ts, tc=tc, gtok=gtok, tgt=vc.n):
                g.wait_ge(vc.h, tgt)
                g.dma_start(out=o2ld.rearrange("p (d f) -> p d f", d=DC),
                            in_=out2_s[ts, :, :, tc * 128:(tc + 1) * 128].rearrange("d p f -> p d f")).then_inc(gd.h, 16)
                g.dma_start(out=x2bF, in_=rr(x2_s[gtok * 128:(gtok + 1) * 128, :])).then_inc(gd.h, 16)
            gd.bump(32)

            @TE
            def _(t, tgt=gd.n, tv=vc.n, tsc=sc.n):
                t.wait_ge(gd.h, tgt)
                t.wait_ge(vc.h, tv)
                t.wait_ge(sc.h, tsc)
                for dcc in range(DC):
                    _mm = t.transpose(rr(PB[:, dcc * 128:(dcc + 1) * 128]),
                                o2ld[:, dcc * 128:(dcc + 1) * 128], identr[:])
                _mm.then_inc(tn.h, 1)
            tn.bump()

            @VE
            def _(v, tgt=tn.n):
                v.wait_ge(tn.h, tgt)
                v.tensor_copy(o2tok, rr(PB[:]))
                v.tensor_copy(vspace[:], zeros8[:]).then_inc(vc.h, 1)
            vc.bump()

            ln_stats(r32(o2tok), D, sqF)
            ln_apply(r32(o2tok), lnO)

            @VE
            def _(v, tgt=sc.n, tg=gd.n):
                v.wait_ge(sc.h, tgt)
                v.wait_ge(gd.h, tg)
                v.tensor_add(lnO, r32(x2bF), r32(lnO))
                v.tensor_copy(vspace[:], zeros8[:]).then_inc(vc.h, 1)
            vc.bump()

            @GE
            def _(g, gtok=gtok, tgt=vc.n):
                g.wait_ge(vc.h, tgt)
                g.dma_start(out=out_t[gtok * 128:(gtok + 1) * 128, :], in_=r32(lnO)).then_inc(st.h, 16)
            st.bump(16)

    @GE
    def _(g, tgt=st.n):
        g.wait_ge(st.h, tgt)

    ctx.close()
    return nc


def _host_layouts(inp):
    f32 = np.float32
    asf = lambda a: np.ascontiguousarray(np.asarray(a, dtype=f32))
    cc = np.ascontiguousarray
    Wq = asf(inp["L0_attn_Wq"]); Wk = asf(inp["L0_attn_Wk"])
    Wv = asf(inp["L0_attn_Wv"]); Wo = asf(inp["L0_attn_Wo"])
    blk = lambda W: cc(W.reshape(DC, 128, DC, 128).transpose(2, 0, 1, 3))
    selz = np.zeros((H, H * 128), dtype=f32)
    for h in range(H):
        selz[h, h * 128:(h + 1) * 128] = 1.0
    sel8 = np.zeros((E, E * 128), dtype=f32)
    for e_ in range(E):
        sel8[e_, e_ * 128:(e_ + 1) * 128] = 1.0
    return {
        "wq_b": blk(Wq), "wk_b": blk(Wk), "wv_b": blk(Wv), "wo_b": blk(Wo),
        "bqB": cc(asf(inp["L0_attn_bq"]).reshape(DC, 128).T),
        "bkB": cc(asf(inp["L0_attn_bk"]).reshape(DC, 128).T),
        "bvB": cc(asf(inp["L0_attn_bv"]).reshape(DC, 128).T),
        "boB": cc(asf(inp["L0_attn_bo"]).reshape(DC, 128).T),
        "gatewT": cc(asf(inp["L0_ffn_gate_w"]).T.reshape(DC, 128, E)),
        "w1T_b": cc(asf(inp["L0_ffn_w1"]).T.reshape(D, FC, 128).transpose(1, 0, 2)),
        "w3T_b": cc(asf(inp["L0_ffn_w3"]).T.reshape(D, FC, 128).transpose(1, 0, 2)),
        "w2T_b": cc(asf(inp["L0_ffn_w2"]).T.reshape(FC, 128, DC, 128).transpose(2, 0, 1, 3)),
        "a1T_b": cc(asf(inp["L0_ffn_a1"]).transpose(2, 0, 1).reshape(D, 128).reshape(DC, 128, 128)),
        "a3T_b": cc(asf(inp["L0_ffn_a3"]).transpose(2, 0, 1).reshape(D, 128).reshape(DC, 128, 128)),
        "a2T_b": cc(asf(inp["L0_ffn_a2"]).transpose(2, 0, 1).reshape(FF, 128).reshape(FC, 128, 128)),
        "b1T_b": cc(asf(inp["L0_ffn_b1"]).reshape(E, FC, 128, R).transpose(1, 0, 3, 2)),
        "b3T_b": cc(asf(inp["L0_ffn_b3"]).reshape(E, FC, 128, R).transpose(1, 0, 3, 2)),
        "b2T_b": cc(asf(inp["L0_ffn_b2"]).reshape(E, DC, 128, R).transpose(1, 0, 3, 2)),
        "identin": np.eye(128, dtype=f32),
        "maskin": np.triu(np.ones((128, 128), dtype=f32)),
        "selzin": selz,
        "sel8in": sel8,
        "onecolin": np.ones((128, 1), dtype=f32),
    }


def kernel(**inputs):
    shared = _host_layouts(inputs)
    x = np.ascontiguousarray(np.asarray(inputs["L0_attn_x"], dtype=np.float32).reshape(SEQ, D))
    nc = build_nc()
    in_maps = []
    for c in range(NCORE):
        m = dict(shared)
        m["x"] = np.ascontiguousarray(x[c * NTOK:(c + 1) * NTOK])
        pm = np.zeros((128, E), np.float32)
        pm[:, :c] = 1.0
        m["prefin"] = pm
        in_maps.append(m)
    res = run_bass_kernel_spmd(nc, in_maps, core_ids=list(range(NCORE)))
    out = np.concatenate([r["out"] for r in res.results], axis=0)
    return out.reshape(1, SEQ, D)


# version 12
# speedup vs baseline: 1.3563x; 1.0301x over previous
import numpy as np
from contextlib import ExitStack

import concourse.bass as bass
import concourse.mybir as mybir
from concourse.bass_utils import run_bass_kernel_spmd

F32 = mybir.dt.float32
F32R = mybir.dt.float32r
AF = mybir.ActivationFunctionType
ALU = mybir.AluOpType
AXX = mybir.AxisListType.X

NCORE = 8
D = 2048
FF = 5632
E = 8
R = 16
H = 16
SEQ = 8192
NTOK = SEQ // NCORE
C = 128
NCH = NTOK // C
NTS = 2
TS = 512
DC = D // 128
FC = FF // 128
EPS_LN = 1e-5
EPS_Z = 1e-6

DEBUG = False
STOP_AFTER = None
A_SUB = 99
EF_SUB = 99
T13_N = 99


class SemC:
    def __init__(self, ctx, nc, name):
        self.h = ctx.enter_context(nc.semaphore(name))
        self.n = 0

    def bump(self, k=1):
        self.n += k
        return self.n


class Arena:
    def __init__(self, tensor, size):
        self.t = tensor
        self.size = size
        self.off = 0

    def reset(self):
        self.off = 0

    def take(self, n, rows=128):
        o = self.off
        assert o + n <= self.size, f"arena overflow {o}+{n} > {self.size}"
        self.off += n
        return self.t[0:rows, o:o + n]


STAGE_ORDER = ["Z", "A", "B", "C1", "CC", "C2", "D", "EF"]

def _stage_on(name):
    if STOP_AFTER is None:
        return True
    return STAGE_ORDER.index(name) <= STAGE_ORDER.index(STOP_AFTER)


def build_nc():
    nc = bass.Bass("TRN2", target_bir_lowering=False)

    dI = lambda name, shape, dt=F32: nc.dram_tensor(name, shape, dt, kind="ExternalInput").ap()
    x_in = dI("x", [NTOK, D])
    wq_b = dI("wq_b", [DC, DC, 128, 128])
    wk_b = dI("wk_b", [DC, DC, 128, 128])
    wv_b = dI("wv_b", [DC, DC, 128, 128])
    wo_b = dI("wo_b", [DC, DC, 128, 128])
    bqB = dI("bqB", [128, DC])
    bkB = dI("bkB", [128, DC])
    bvB = dI("bvB", [128, DC])
    boB = dI("boB", [128, DC])
    gatewT = dI("gatewT", [DC, 128, E])
    w1T_b = dI("w1T_b", [FC, D, 128])
    w3T_b = dI("w3T_b", [FC, D, 128])
    w2T_b = dI("w2T_b", [DC, FC, 128, 128])
    a1T_b = dI("a1T_b", [DC, 128, 128])
    a3T_b = dI("a3T_b", [DC, 128, 128])
    a2T_b = dI("a2T_b", [FC, 128, 128])
    b1T_b = dI("b1T_b", [FC, E, R, 128])
    b3T_b = dI("b3T_b", [FC, E, R, 128])
    b2T_b = dI("b2T_b", [DC, E, R, 128])
    identin = dI("identin", [128, 128])
    maskin = dI("maskin", [128, 128])
    selzin = dI("selzin", [H, H * 128])
    sel8in = dI("sel8in", [E, E * 128])
    onecolin = dI("onecolin", [128, 1])
    prefin = dI("prefin", [128, E])

    out_t = nc.dram_tensor("out", [NTOK, D], F32, kind="ExternalOutput").ap()

    skind = "ExternalOutput" if DEBUG else "Internal"
    dS = lambda name, shape, dt=F32R: nc.dram_tensor(name, shape, dt, kind=skind).ap()
    qfT_s = dS("qfT_s", [NCH, H, 128, 128])
    kfT_s = dS("kfT_s", [NCH, H, 128, 128])
    vT_s = dS("vT_s", [NCH, H, 128, 128])
    aoT_s = dS("aoT_s", [NCH, DC, 128, 128])
    x2_s = dS("x2_s", [NTOK, D], F32)
    xl2T_s = dS("xl2T_s", [DC, 128, NTOK])
    wacc_s = dS("wacc_s", [NTS, FC, 128, TS])
    out2_s = dS("out2_s", [NTS, DC, 128, TS])
    xln2dbg_s = dS("xln2dbg_s", [NCH, 128, D])
    xl2dbg2_s = dS("xl2dbg2_s", [NCH, 128, D])
    ffdbg_s = dS("ffdbg_s", [8, 128, TS])
    wbdbg_s = dS("wbdbg_s", [128, E * TS])
    stdbg_s = dS("stdbg_s", [2, 128, D + H])
    cc_sz = H * 128 * 128 + H * 128
    cc_in = nc.dram_tensor("cc_in", [1, cc_sz], F32).ap()
    cc_out = nc.dram_tensor("cc_out", [NCORE, cc_sz], F32).ap()

    ctx = ExitStack()
    sb = lambda name, shape, dt: ctx.enter_context(nc.sbuf_tensor(name, shape, dt))

    identr = sb("identr", [128, 128], F32R)
    ident32 = sb("ident32", [128, 128], F32)
    maskut = sb("maskut", [128, 128], F32)
    selz = sb("selz", [H, H * 128], F32R)
    sel8 = sb("sel8", [E, E * 128], F32R)
    onecol = sb("onecol", [128, 1], F32R)
    bq_t = sb("bq_t", [128, DC], F32)
    bk_t = sb("bk_t", [128, DC], F32)
    bv_t = sb("bv_t", [128, DC], F32)
    bo_t = sb("bo_t", [128, DC], F32)
    gw_t = sb("gw_t", [128, DC * E], F32R)
    prefm = sb("prefm", [128, E], F32)
    epsln = sb("epsln", [128, 1], F32)
    zeros8 = sb("zeros8", [128, 8], F32)
    zerosT = sb("zerosT", [128, TS], F32)
    stat = sb("stat", [128, 12], F32)
    tp2 = sb("tp2", [128, 10], F32)
    tp8 = sb("tp8", [128, 40], F32)
    kredc = sb("kredc", [128, H], F32)
    ktot = sb("ktot", [128, 2 * H], F32)
    lgt = sb("lgt", [128, E], F32)
    zrow = sb("zrow", [1, 128], F32)
    vspace = sb("vspace", [128, 8], F32)
    tpeq = sb("tpeq", [128, 16], mybir.dt.int32)
    sspace = sb("sspace", [128, 8], F32)

    ARENA_N = 47000
    UE_N = 4096
    arena_t = sb("arena", [128, ARENA_N], F32R)
    A = Arena(arena_t, ARENA_N - UE_N)
    ueT = arena_t[0:R, ARENA_N - UE_N:ARENA_N]

    PA = ctx.enter_context(nc.psum_tensor("PA", [128, 2048], F32))
    PB = ctx.enter_context(nc.psum_tensor("PB", [128, 2048], F32))

    wd = SemC(ctx, nc, "wd")
    gd = SemC(ctx, nc, "gd")
    tn = SemC(ctx, nc, "tn")
    vc = SemC(ctx, nc, "vc")
    sc = SemC(ctx, nc, "sc")
    gp = SemC(ctx, nc, "gp")
    st = SemC(ctx, nc, "st")

    ctx.enter_context(nc.allow_low_precision("deliberate f32r (TF32-grade) on-chip storage"))
    block = ctx.enter_context(nc.Block())
    SY = block.sync
    TE = block.tensor
    VE = block.vector
    SE = block.scalar
    GE = block.gpsimd

    r32 = lambda ap: ap.bitcast(F32)
    rr = lambda ap: ap.bitcast(F32R)

    def ln_stats(xap32, n_feat, sq_scr):
        @VE
        def _(v):
            v.reduce_sum(stat[:, 0:1], xap32, axis=AXX, negate=True)
            v.tensor_copy(vspace[:], zeros8[:]).then_inc(vc.h, 1)
        vc.bump()

        @SE
        def _(s, tgt=vc.n):
            s.wait_ge(vc.h, tgt)
            s.activation(out=sq_scr, in_=xap32, func=AF.Square, accum_out=stat[:, 1:2])
            s.activation(out=sspace[:], in_=zeros8[:], func=AF.Identity).then_inc(sc.h, 1)
        sc.bump()

        @VE
        def _(v, tgt=sc.n):
            v.wait_ge(sc.h, tgt)
            v.tensor_scalar(out=stat[:, 2:3], in0=stat[:, 0:1], scalar1=1.0 / n_feat, scalar2=None, op0=ALU.mult)
            v.engine_nop()
            v.engine_nop()
            v.engine_nop()
            v.engine_nop()
            v.tensor_tensor(stat[:, 3:4], stat[:, 2:3], stat[:, 2:3], op=ALU.mult)
            v.tensor_scalar(out=stat[:, 4:5], in0=stat[:, 1:2], scalar1=1.0 / n_feat, scalar2=None, op0=ALU.mult)
            v.engine_nop()
            v.engine_nop()
            v.engine_nop()
            v.engine_nop()
            v.tensor_sub(stat[:, 4:5], stat[:, 4:5], stat[:, 3:4])
            v.tensor_copy(vspace[:], zeros8[:]).then_inc(vc.h, 1)
        vc.bump()

        @SE
        def _(s, tgt=vc.n):
            s.wait_ge(vc.h, tgt)
            s.activation(out=stat[:, 5:6], in_=stat[:, 4:5], func=AF.Sqrt, bias=epsln[:])
            s.activation(out=sspace[:], in_=zeros8[:], func=AF.Identity).then_inc(sc.h, 1)
        sc.bump()

        @VE
        def _(v, tgt=sc.n):
            v.wait_ge(sc.h, tgt)
            v.reciprocal(stat[:, 6:7], stat[:, 5:6])
            v.engine_nop()
            v.engine_nop()
            v.engine_nop()
            v.engine_nop()
            v.tensor_tensor(stat[:, 7:8], stat[:, 2:3], stat[:, 6:7], op=ALU.mult)
            v.tensor_copy(vspace[:], zeros8[:]).then_inc(vc.h, 1)
        vc.bump()

    def ln_apply(xap32, outap):
        @SE
        def _(s, tgt=vc.n):
            s.wait_ge(vc.h, tgt)
            s.activation(out=outap, in_=xap32, func=AF.Identity, scale=stat[:, 6:7], bias=stat[:, 7:8])
            s.activation(out=sspace[:], in_=zeros8[:], func=AF.Identity).then_inc(sc.h, 1)
        sc.bump()

    @SY
    def _(e):
        for dst, src, cast in [
            (identr, identin, True), (ident32, identin, False), (maskut, maskin, False),
            (selz, selzin, True), (sel8, sel8in, True), (onecol, onecolin, True),
            (bq_t, bqB, False), (bk_t, bkB, False), (bv_t, bvB, False), (bo_t, boB, False),
            (prefm, prefin, False),
        ]:
            e.dma_start(out=dst[:], in_=(rr(src) if cast else src)).then_inc(wd.h, 16)
            wd.bump(16)
        e.dma_start(out=gw_t[:].rearrange("p (k f) -> p k f", k=DC),
                    in_=rr(gatewT).rearrange("k p f -> p k f")).then_inc(wd.h, 16)
        wd.bump(16)

    @GE
    def _(g):
        g.memset(epsln[:], EPS_LN)
        g.memset(zeros8[:], 0.0)
        g.memset(zerosT[:], 0.0)
        g.memset(ktot[:], 0.0)
        g.sem_inc(gp.h, 1)
    gp.bump()

    A.reset()
    xb = [A.take(D) for _ in range(2)]
    sq_scr = A.take(D)
    xln = A.take(D)
    xlnT = A.take(DC * NTOK)
    wst = [A.take(D) for _ in range(2)]
    m0b = A.take(TS)
    p0b = A.take(TS)
    e0b = A.take(TS)
    qev = [A.take(TS) for _ in range(2)]
    xlnT_v = xlnT.rearrange("p (d t) -> p d t", d=DC)
    assert A.off <= ARENA_N - UE_N

    @VE
    def _(v, tgt=gp.n):
        v.wait_ge(gp.h, tgt)
    @SE
    def _(s, tgt=gp.n):
        s.wait_ge(gp.h, tgt)
    @TE
    def _(t, tgt=wd.n):
        t.wait_ge(wd.h, tgt)

    for c in range(NCH if _stage_on('A') else 0):
        @SY
        def _(e, c=c, tgt=sc.n):
            e.wait_ge(sc.h, tgt)
            e.dma_start(out=xb[c % 2], in_=rr(x_in[c * 128:(c + 1) * 128, :])).then_inc(wd.h, 16)
        wd.bump(16)

        @VE
        def _(v, tgt=wd.n):
            v.wait_ge(wd.h, tgt)
        @SE
        def _(s, tgt=wd.n):
            s.wait_ge(wd.h, tgt)

        if A_SUB >= 2:
            ln_stats(r32(xb[c % 2]), D, sq_scr)
        if A_SUB >= 3:
            ln_apply(r32(xb[c % 2]), xln)

        if A_SUB >= 4:
            @TE
            def _(t, tgt=sc.n, tv=vc.n):
                t.wait_ge(sc.h, tgt)
                t.wait_ge(vc.h, tv)
                for k in range(DC):
                    _mm = t.transpose(rr(PA[:, k * 128:(k + 1) * 128]), xln[:, k * 128:(k + 1) * 128], identr[:])
                _mm.then_inc(tn.h, 1)
            tn.bump()

        if A_SUB >= 5:
            @VE
            def _(v, c=c, tgt=tn.n):
                v.wait_ge(tn.h, tgt)
                v.tensor_copy(xlnT_v[:, :, c * 128:(c + 1) * 128], rr(PA[:]).rearrange("p (d t) -> p d t", d=DC))
                v.tensor_copy(vspace[:], zeros8[:]).then_inc(vc.h, 1)
            vc.bump()

    projs = [(wq_b, bq_t, qfT_s, True), (wk_b, bk_t, kfT_s, True), (wv_b, bv_t, vT_s, False)]
    wst_use = [tn.n, tn.n]
    slot_vc = [vc.n] * 4
    slot_sc = [sc.n] * 4
    for wbl, bias_t, dst_s, is_elu in (projs if _stage_on('B') else []):
        for dq in range(DC):
            @SY
            def _(e, wbl=wbl, dq=dq, tgt=wst_use[dq % 2]):
                e.wait_ge(tn.h, tgt)
                e.dma_start(out=wst[dq % 2].rearrange("p (k f) -> p k f", k=DC),
                            in_=rr(wbl[dq]).rearrange("k p f -> p k f")).then_inc(wd.h, 16)
            wd.bump(16)

            for ts in range(NTS):
                slot = (2 * dq + ts) % 4
                pa_sl = PA[:, slot * 512:(slot + 1) * 512]

                @TE
                def _(t, dq=dq, ts=ts, pa_sl=pa_sl, tgt=wd.n, tv=slot_vc[slot], tsc=slot_sc[slot]):
                    t.wait_ge(wd.h, tgt)
                    t.wait_ge(vc.h, tv)
                    t.wait_ge(sc.h, tsc)
                    for k in range(DC):
                        _mm = t.matmul(pa_sl, wst[dq % 2][:, k * 128:(k + 1) * 128],
                                 xlnT_v[:, k, ts * TS:(ts + 1) * TS],
                                 start=(k == 0), stop=(k == DC - 1))
                    _mm.then_inc(tn.h, 1)
                tn.bump()
                wst_use[dq % 2] = tn.n

                if is_elu:
                    @VE
                    def _(v, pa_sl=pa_sl, dq=dq, bias_t=bias_t, tgt=tn.n):
                        v.wait_ge(tn.h, tgt)
                        v.scalar_tensor_tensor(out=m0b, in0=pa_sl, scalar=bias_t[:, dq:dq + 1],
                                               in1=zerosT[:], op0=ALU.add, op1=ALU.min)
                        v.tensor_copy(vspace[:], zeros8[:]).then_inc(vc.h, 1)
                    vc.bump()

                    @SE
                    def _(s, pa_sl=pa_sl, dq=dq, bias_t=bias_t, tgt=vc.n, ttn=tn.n):
                        s.wait_ge(tn.h, ttn)
                        s.activation(out=p0b, in_=pa_sl, func=AF.Relu, bias=bias_t[:, dq:dq + 1])
                        s.wait_ge(vc.h, tgt)
                        s.activation(out=e0b, in_=r32(m0b), func=AF.Exp)
                        s.activation(out=sspace[:], in_=zeros8[:], func=AF.Identity).then_inc(sc.h, 1)
                    sc.bump()

                    @VE
                    def _(v, ts=ts, tgt=sc.n, tg=gd.n):
                        v.wait_ge(sc.h, tgt)
                        v.wait_ge(gd.h, tg)
                        v.tensor_add(qev[ts % 2], r32(e0b), r32(p0b))
                        v.tensor_copy(vspace[:], zeros8[:]).then_inc(vc.h, 1)
                    vc.bump()
                    evac_sem, evac_tgt = vc, vc.n
                else:
                    @SE
                    def _(s, pa_sl=pa_sl, dq=dq, ts=ts, bias_t=bias_t, tgt=tn.n, tg=gd.n):
                        s.wait_ge(tn.h, tgt)
                        s.wait_ge(gd.h, tg)
                        s.activation(out=qev[ts % 2], in_=pa_sl, func=AF.Identity,
                                     bias=bias_t[:, dq:dq + 1])
                        s.activation(out=sspace[:], in_=zeros8[:], func=AF.Identity).then_inc(sc.h, 1)
                    sc.bump()
                    evac_sem, evac_tgt = sc, sc.n

                slot_vc[slot] = vc.n
                slot_sc[slot] = sc.n

                @GE
                def _(g, tgt=evac_tgt, sh=evac_sem.h, dq=dq, ts=ts, dst_s=dst_s):
                    g.wait_ge(sh, tgt)
                    g.dma_start(out=dst_s[ts * 4:(ts + 1) * 4, dq].rearrange("c p f -> p c f"),
                                in_=qev[ts % 2].rearrange("p (c f) -> p c f", c=4)).then_inc(gd.h, 16)
                gd.bump(16)

    A.reset()
    qft = [A.take(D) for _ in range(2)]
    kft = [A.take(D) for _ in range(2)]
    vtt = [A.take(D) for _ in range(2)]
    kftok = A.take(D)
    vtok = A.take(D)
    Sacc = [A.take(D + H) for _ in range(2)]
    atm = [A.take(128) for _ in range(2)]
    zall = A.take(128, rows=H)
    zbsb = A.take(128)
    aoTsb = A.take(D)
    stot = A.take(D)
    ccld = [A.take(D + H) for _ in range(2)]
    assert A.off <= ARENA_N - UE_N

    @GE
    def _(g, tgt=gd.n):
        g.wait_ge(gd.h, tgt)
        g.sem_inc(gp.h, 1)
    gp.bump()

    @VE
    def _(v, tgt=gp.n):
        v.wait_ge(gp.h, tgt)
        v.tensor_copy(zall, zerosT[0:H, 0:128])
        v.tensor_copy(vspace[:], zeros8[:]).then_inc(vc.h, 1)
    vc.bump()

    def load_chunk_qkv(c, want_q):
        @GE
        def _(g, c=c, tgt=tn.n, tv=vc.n, b=c % 2):
            g.wait_ge(tn.h, tgt)
            g.wait_ge(vc.h, tv)
            if want_q:
                g.dma_start(out=qft[b].rearrange("p (h f) -> p h f", h=H),
                            in_=qfT_s[c].rearrange("h p f -> p h f")).then_inc(gd.h, 16)
            g.dma_start(out=kft[b].rearrange("p (h f) -> p h f", h=H),
                        in_=kfT_s[c].rearrange("h p f -> p h f")).then_inc(gd.h, 16)
            g.dma_start(out=vtt[b].rearrange("p (h f) -> p h f", h=H),
                        in_=vT_s[c].rearrange("h p f -> p h f")).then_inc(gd.h, 16)
        gd.bump(16 * (3 if want_q else 2))

    def transpose_to_tok(srcbuf, dstbuf):
        @TE
        def _(t, srcbuf=srcbuf, tgt=gd.n, tv=vc.n):
            t.wait_ge(gd.h, tgt)
            t.wait_ge(vc.h, tv)
            for h in range(H):
                _mm = t.transpose(rr(PA[:, h * 128:(h + 1) * 128]), srcbuf[:, h * 128:(h + 1) * 128], identr[:])
            _mm.then_inc(tn.h, 1)
        tn.bump()

        @VE
        def _(v, dstbuf=dstbuf, tgt=tn.n, tg=gd.n):
            v.wait_ge(tn.h, tgt)
            v.wait_ge(gd.h, tg)
            v.tensor_copy(dstbuf, rr(PA[:]))
            v.tensor_copy(vspace[:], zeros8[:]).then_inc(vc.h, 1)
        vc.bump()

    @TE
    def _(t, tgt=vc.n):
        t.wait_ge(vc.h, tgt)
        for bk in range(4):
            _mm = t.matmul(PB[:, bk * 512:bk * 512 + 16], ident32[:], zerosT[:, 0:16],
                           start=True, stop=True, skip_group_check=True)
        _mm.then_inc(tn.h, 1)
    tn.bump()

    for c in range(NCH if _stage_on('C1') else 0):
        load_chunk_qkv(c, want_q=False)
        transpose_to_tok(kft[c % 2], kftok)
        transpose_to_tok(vtt[c % 2], vtok)

        @TE
        def _(t, c=c, tgt=vc.n):
            t.wait_ge(vc.h, tgt)
            for h in range(H):
                _mm = t.matmul(PB[:, h * 128:(h + 1) * 128], kftok[:, h * 128:(h + 1) * 128],
                         vtok[:, h * 128:(h + 1) * 128], start=False, stop=(c == NCH - 1),
                         skip_group_check=True)
            _mm.then_inc(tn.h, 1)
        tn.bump()

        @VE
        def _(v, c=c, tgt=gp.n, b=c % 2):
            v.wait_ge(gp.h, tgt)
            for h in range(H):
                v.reduce_sum(kredc[:, h:h + 1], r32(kft[b][:, h * 128:(h + 1) * 128]), axis=AXX)
            v.engine_nop()
            v.engine_nop()
            v.engine_nop()
            v.engine_nop()
            v.tensor_add(ktot[:, ((c + 1) % 2) * H:((c + 1) % 2 + 1) * H],
                         ktot[:, (c % 2) * H:(c % 2 + 1) * H], kredc[:])
            v.tensor_copy(vspace[:], zeros8[:]).then_inc(vc.h, 1)
        vc.bump()

    if _stage_on('CC'):
        @VE
        def _(v, tgt=tn.n):
            v.wait_ge(tn.h, tgt)
            v.tensor_copy(stot, PB[:])
            v.tensor_copy(vspace[:], zeros8[:]).then_inc(vc.h, 1)
        vc.bump()

        @GE
        def _(g, tgt=vc.n, base=gd.n):
            g.wait_ge(vc.h, tgt)
            g.dma_start(out=cc_in[0:1, 0:H * 128 * 128].rearrange("o (p n) -> (o p) n", p=128),
                        in_=r32(stot)).then_inc(gd.h, 16)
            g.dma_start(out=cc_in[0:1, H * 128 * 128:].rearrange("o (p n) -> (o p) n", p=128),
                        in_=ktot[:, (NCH % 2) * H:(NCH % 2 + 1) * H]).then_inc(gd.h, 16)
            g.wait_ge(gd.h, base + 32)
            g.collective_compute(
                "AllGather", ALU.bypass,
                replica_groups=[list(range(NCORE))],
                ins=[cc_in.opt()],
                outs=[cc_out.opt()],
            ).then_inc(gp.h, 1)
        gd.bump(32)
        gp.bump()

        @VE
        def _(v):
            for j in range(4):
                v.tensor_copy(Sacc[0][:, j * 512:(j + 1) * 512], zerosT[:])
            v.tensor_copy(Sacc[0][:, 4 * 512:4 * 512 + H], zerosT[:, 0:H])
            v.tensor_copy(vspace[:], zeros8[:]).then_inc(vc.h, 1)
        vc.bump()

        for cp in range(NCORE):
            @GE
            def _(g, cp=cp, tgt=gp.n, tv=vc.n, b=cp % 2):
                g.wait_ge(gp.h, tgt)
                g.wait_ge(vc.h, tv)
                g.dma_start(out=ccld[b][:, 0:D],
                            in_=rr(cc_out[cp:cp + 1, 0:H * 128 * 128]).rearrange("o (p n) -> (o p) n", p=128)).then_inc(gd.h, 16)
                g.dma_start(out=ccld[b][:, D:D + H],
                            in_=rr(cc_out[cp:cp + 1, H * 128 * 128:]).rearrange("o (p n) -> (o p) n", p=128)).then_inc(gd.h, 16)
            gd.bump(32)

            @VE
            def _(v, cp=cp, tgt=gd.n, b=cp % 2):
                v.wait_ge(gd.h, tgt)
                v.scalar_tensor_tensor(out=Sacc[0], in0=r32(ccld[b]), scalar=prefm[:, cp:cp + 1],
                                       in1=r32(Sacc[0]), op0=ALU.mult, op1=ALU.add)
                v.tensor_copy(vspace[:], zeros8[:]).then_inc(vc.h, 1)
            vc.bump()


    if DEBUG:
        @GE
        def _(g, tgt=vc.n):
            g.wait_ge(vc.h, tgt)
            g.dma_start(out=stdbg_s[0, :, 0:D], in_=r32(stot)).then_inc(gd.h, 16)
            g.dma_start(out=stdbg_s[0, :, D:D + H], in_=ktot[:, (NCH % 2) * H:(NCH % 2 + 1) * H]).then_inc(gd.h, 16)
            g.dma_start(out=stdbg_s[1], in_=r32(Sacc[0])).then_inc(gd.h, 16)
        gd.bump(48)

    for c in range(NCH if _stage_on('C2') else 0):
        cur, nxt = Sacc[c % 2], Sacc[(c + 1) % 2]
        Srun, ksrun = cur[:, 0:D], cur[:, D:D + H]
        Snxt, ksnxt = nxt[:, 0:D], nxt[:, D:D + H]

        load_chunk_qkv(c, want_q=True)
        transpose_to_tok(kft[c % 2], kftok)
        transpose_to_tok(vtt[c % 2], vtok)

        for h in range(H):
            at_sl = PA[:, 0:128] if h % 2 == 0 else PA[:, 512:640]
            io_sl = PA[:, 1024:1152]
            z_sl = PA[0:1, 1536:1664]
            zb_sl = PA[:, 1664:1792]
            kv_sl = PB[:, 0:128] if h % 2 == 0 else PB[:, 512:640]
            qsl = qft[c % 2][:, h * 128:(h + 1) * 128]
            kfsl = kft[c % 2][:, h * 128:(h + 1) * 128]
            ktsl = kftok[:, h * 128:(h + 1) * 128]
            vtsl = vtok[:, h * 128:(h + 1) * 128]
            ab = atm[h % 2]

            @TE
            def _(t, at_sl=at_sl, kfsl=kfsl, qsl=qsl, tgt=vc.n):
                t.wait_ge(vc.h, tgt)
                _mm = t.matmul(at_sl, kfsl, qsl, start=True, stop=True)
                _mm.then_inc(tn.h, 1)
            tn.bump()

            @VE
            def _(v, at_sl=at_sl, ab=ab, tgt=tn.n):
                v.wait_ge(tn.h, tgt)
                v.tensor_tensor(ab, at_sl, maskut[:], op=ALU.mult)
                v.tensor_copy(vspace[:], zeros8[:]).then_inc(vc.h, 1)
            vc.bump()

            @TE
            def _(t, io_sl=io_sl, z_sl=z_sl, kv_sl=kv_sl, h=h, ab=ab,
                  qsl=qsl, vtsl=vtsl, ktsl=ktsl, Srun=Srun, ksrun=ksrun, tgt=vc.n):
                t.wait_ge(vc.h, tgt)
                t.matmul(io_sl, Srun[:, h * 128:(h + 1) * 128], qsl, start=True, stop=False)
                t.matmul(io_sl, vtsl, ab, start=False, stop=True)
                t.matmul(z_sl, ksrun[:, h:h + 1], qsl, start=True, stop=False)
                t.matmul(z_sl, onecol[:], ab, start=False, stop=True)
                _mm = t.matmul(kv_sl, ktsl, vtsl, start=True, stop=True)
                _mm.then_inc(tn.h, 1)
            tn.bump()

            @VE
            def _(v, z_sl=z_sl, h=h, tgt=tn.n):
                v.wait_ge(tn.h, tgt)
                v.tensor_scalar(out=zrow[:], in0=z_sl, scalar1=EPS_Z, scalar2=None, op0=ALU.add)
                v.engine_nop()
                v.engine_nop()
                v.engine_nop()
                v.engine_nop()
                v.reciprocal(zall[0:1, :], zrow[:])
                v.tensor_copy(vspace[:], zeros8[:]).then_inc(vc.h, 1)
            vc.bump()

            @TE
            def _(t, zb_sl=zb_sl, h=h, tgt=vc.n):
                t.wait_ge(vc.h, tgt)
                _mm = t.matmul(zb_sl, selz[:, 0:128], zall[:], start=True, stop=True)
                _mm.then_inc(tn.h, 1)
            tn.bump()

            @VE
            def _(v, zb_sl=zb_sl, io_sl=io_sl, kv_sl=kv_sl, h=h, kfsl=kfsl,
                  Srun=Srun, Snxt=Snxt, ksrun=ksrun, ksnxt=ksnxt, tgt=tn.n):
                v.wait_ge(tn.h, tgt)
                v.tensor_copy(zbsb, zb_sl)
                v.engine_nop()
                v.engine_nop()
                v.engine_nop()
                v.engine_nop()
                v.tensor_tensor(aoTsb[:, h * 128:(h + 1) * 128], io_sl, r32(zbsb), op=ALU.mult)
                v.tensor_add(Snxt[:, h * 128:(h + 1) * 128], r32(Srun[:, h * 128:(h + 1) * 128]), kv_sl)
                v.reduce_sum(kredc[:, h:h + 1], r32(kfsl), axis=AXX)
                v.engine_nop()
                v.engine_nop()
                v.engine_nop()
                v.engine_nop()
                v.tensor_add(ksnxt[:, h:h + 1], r32(ksrun[:, h:h + 1]), kredc[:, h:h + 1])
                v.tensor_copy(vspace[:], zeros8[:]).then_inc(vc.h, 1)
            vc.bump()

        @GE
        def _(g, c=c, tgt=vc.n):
            g.wait_ge(vc.h, tgt)
            g.dma_start(out=aoT_s[c].rearrange("h p f -> p h f"),
                        in_=aoTsb.rearrange("p (h f) -> p h f", h=DC)).then_inc(gd.h, 16)
        gd.bump(16)

    A.reset()
    aot = A.take(DC * TS)
    ybufT = A.take(DC * TS)
    wstD = [A.take(D) for _ in range(2)]
    ytok = A.take(D)
    lnY = A.take(D)
    x2b = A.take(D)
    xln2 = A.take(D)
    sqD = A.take(D)
    xstage = A.take(D)
    xl2stg = A.take(D)
    aot_v = aot.rearrange("p (h t) -> p h t", h=DC)
    assert A.off <= ARENA_N - UE_N

    for ts in range(NTS if _stage_on('D') else 0):
        @GE
        def _(g, ts=ts, tgt=tn.n, tv=vc.n):
            g.wait_ge(tn.h, tgt)
            g.wait_ge(vc.h, tv)
            for fch in range(DC):
                g.dma_start(out=aot_v[:, fch, :].rearrange("p (c f) -> p c f", c=4),
                            in_=aoT_s[ts * 4:(ts + 1) * 4, fch].rearrange("c p f -> p c f")).then_inc(gd.h, 16)
        gd.bump(16 * DC)

        wd_use = [tn.n, tn.n]
        for jc in range(DC):
            @SY
            def _(e, jc=jc, tgt=wd_use[jc % 2]):
                e.wait_ge(tn.h, tgt)
                e.dma_start(out=wstD[jc % 2].rearrange("p (k f) -> p k f", k=DC),
                            in_=rr(wo_b[jc]).rearrange("k p f -> p k f")).then_inc(wd.h, 16)
            wd.bump(16)

            pa_sl = PA[:, (jc % 2) * 512:(jc % 2) * 512 + 512]

            @TE
            def _(t, jc=jc, pa_sl=pa_sl, tgt=wd.n, tg=gd.n, tsc=sc.n):
                t.wait_ge(wd.h, tgt)
                t.wait_ge(gd.h, tg)
                t.wait_ge(sc.h, tsc)
                for k in range(DC):
                    _mm = t.matmul(pa_sl, wstD[jc % 2][:, k * 128:(k + 1) * 128], aot_v[:, k, :],
                             start=(k == 0), stop=(k == DC - 1))
                _mm.then_inc(tn.h, 1)
            tn.bump()
            wd_use[jc % 2] = tn.n

            @SE
            def _(s, jc=jc, pa_sl=pa_sl, tgt=tn.n):
                s.wait_ge(tn.h, tgt)
                s.activation(out=ybufT[:, jc * 512:(jc + 1) * 512], in_=pa_sl, func=AF.Identity,
                             bias=bo_t[:, jc:jc + 1])
                s.activation(out=sspace[:], in_=zeros8[:], func=AF.Identity).then_inc(sc.h, 1)
            sc.bump()

        for tc in range(4):
            gtok = ts * 4 + tc

            @TE
            def _(t, tc=tc, tgt=sc.n, tv=vc.n):
                t.wait_ge(sc.h, tgt)
                t.wait_ge(vc.h, tv)
                for jc in range(DC):
                    _mm = t.transpose(rr(PB[:, jc * 128:(jc + 1) * 128]),
                                ybufT[:, jc * 512 + tc * 128: jc * 512 + (tc + 1) * 128], identr[:])
                _mm.then_inc(tn.h, 1)
            tn.bump()

            @VE
            def _(v, tgt=tn.n):
                v.wait_ge(tn.h, tgt)
                v.tensor_copy(ytok, rr(PB[:]))
                v.tensor_copy(vspace[:], zeros8[:]).then_inc(vc.h, 1)
            vc.bump()

            @SY
            def _(e, gtok=gtok, tgt=vc.n):
                e.wait_ge(vc.h, tgt)
                e.dma_start(out=xstage, in_=rr(x_in[gtok * 128:(gtok + 1) * 128, :])).then_inc(wd.h, 16)
            wd.bump(16)

            ln_stats(r32(ytok), D, sqD)
            ln_apply(r32(ytok), lnY)

            @VE
            def _(v, tgt=sc.n, tw=wd.n):
                v.wait_ge(sc.h, tgt)
                v.wait_ge(wd.h, tw)
                v.tensor_add(x2b, r32(xstage), r32(lnY))
                v.tensor_copy(vspace[:], zeros8[:]).then_inc(vc.h, 1)
            vc.bump()

            @GE
            def _(g, gtok=gtok, tgt=vc.n):
                g.wait_ge(vc.h, tgt)
                g.dma_start(out=x2_s[gtok * 128:(gtok + 1) * 128, :], in_=r32(x2b)).then_inc(gd.h, 16)
            gd.bump(16)

            ln_stats(r32(x2b), D, sqD)
            ln_apply(r32(x2b), xln2)

            @SE
            def _(s):
                _ls = s.activation(out=sqD, in_=r32(x2b), func=AF.Square)
                _ls
                s.activation(out=sspace[:], in_=zeros8[:], func=AF.Identity).then_inc(sc.h, 1)
            sc.bump()

            if DEBUG:
                @GE
                def _(g, gtok=gtok, tgt=sc.n):
                    g.wait_ge(sc.h, tgt)
                    g.dma_start(out=xln2dbg_s[gtok], in_=xln2).then_inc(gd.h, 16)
                gd.bump(16)

            @TE
            def _(t, tgt=sc.n, tv=vc.n):
                t.wait_ge(sc.h, tgt)
                t.wait_ge(vc.h, tv)
                for k in range(DC):
                    _mm = t.transpose(rr(PB[:, k * 128:(k + 1) * 128]), xln2[:, k * 128:(k + 1) * 128], identr[:])
                _mm.then_inc(tn.h, 1)
            tn.bump()

            if DEBUG:
                @VE
                def _(v, tgt=tn.n):
                    v.wait_ge(tn.h, tgt)
                    v.tensor_copy(sqD, rr(PB[:]))
                    v.tensor_copy(vspace[:], zeros8[:]).then_inc(vc.h, 1)
                vc.bump()

                @GE
                def _(g, gtok=gtok, tgt=vc.n):
                    g.wait_ge(vc.h, tgt)
                    g.dma_start(out=xl2dbg2_s[gtok], in_=sqD).then_inc(gd.h, 16)
                gd.bump(16)

            @VE
            def _(v, tgt=tn.n, tg=gd.n):
                v.wait_ge(tn.h, tgt)
                v.wait_ge(gd.h, tg)
                v.tensor_copy(xl2stg, rr(PB[:]))
                v.tensor_copy(vspace[:], zeros8[:]).then_inc(vc.h, 1)
            vc.bump()

            @GE
            def _(g, gtok=gtok, tgt=vc.n):
                g.wait_ge(vc.h, tgt)
                g.dma_start(out=xl2T_s[:, :, gtok * 128:(gtok + 1) * 128].rearrange("d p t -> p d t"),
                            in_=xl2stg.rearrange("p (d t) -> p d t", d=DC)).then_inc(gd.h, 16)
            gd.bump(16)

    A.reset()
    xl2t = A.take(DC * TS)
    a1t = A.take(D)
    a3t = A.take(D)
    w1s = [A.take(D) for _ in range(2)]
    w3s = [A.take(D) for _ in range(2)]
    b1s = [A.take(E * 128, rows=R) for _ in range(2)]
    b3s = [A.take(E * 128, rows=R) for _ in range(2)]
    a2s = [A.take(128) for _ in range(2)]
    base1sb = A.take(TS)
    base3sb = A.take(TS)
    ssb = A.take(TS)
    pwsb = A.take(TS)
    pwwsb = [A.take(TS) for _ in range(2)]
    Wb = A.take(E * TS)
    t1c = [A.take(TS, rows=R) for _ in range(E)]
    t3c = [A.take(TS, rows=R) for _ in range(E)]
    lgE = A.take(TS, rows=E)
    w8b = A.take(E)
    w8T = A.take(TS, rows=E)
    waccsb = [A.take(TS) for _ in range(2)]
    usb = A.take(TS)
    assert A.off <= ARENA_N - UE_N, A.off
    xl2t_v = xl2t.rearrange("p (d t) -> p d t", d=DC)

    A.reset()
    w2sF = A.take(FC * 128)
    b2sF = A.take(E * 128, rows=R)
    waccR = A.take(FC * TS)
    o2ld = A.take(D)
    o2tok = A.take(D)
    lnO = A.take(D)
    x2bF = A.take(D)
    sqF = A.take(D)
    o2st = A.take(TS)
    assert A.off <= ARENA_N - UE_N, A.off
    w2sF_v = w2sF.rearrange("p (k f) -> p k f", k=FC)
    waccR_v = waccR.rearrange("p (k t) -> p k t", k=FC)

    for ts in range(NTS if _stage_on('EF') else 0):
        @GE
        def _(g, ts=ts, tgt=tn.n, tv=vc.n, tsc=sc.n):
            g.wait_ge(tn.h, tgt)
            g.wait_ge(vc.h, tv)
            g.wait_ge(sc.h, tsc)
            for dd in range(DC):
                g.dma_start(out=xl2t_v[:, dd, :],
                            in_=xl2T_s[dd, :, ts * TS:(ts + 1) * TS]).then_inc(gd.h, 16)
        gd.bump(16 * DC)

        @SY
        def _(e, tgt=tn.n):
            e.wait_ge(tn.h, tgt)
            e.dma_start(out=a1t.rearrange("p (k f) -> p k f", k=DC),
                        in_=rr(a1T_b).rearrange("k p f -> p k f")).then_inc(wd.h, 16)
            e.dma_start(out=a3t.rearrange("p (k f) -> p k f", k=DC),
                        in_=rr(a3T_b).rearrange("k p f -> p k f")).then_inc(wd.h, 16)
        wd.bump(32)

        if EF_SUB >= 2:
            @TE
            def _(t, tgt=gd.n, tw=wd.n, tv=vc.n):
                t.wait_ge(gd.h, tgt)
                t.wait_ge(wd.h, tw)
                t.wait_ge(vc.h, tv)
                for k in range(DC):
                    _mm = t.matmul(PB[0:E, 1536:1536 + TS], gw_t[:, k * E:(k + 1) * E], xl2t_v[:, k, :],
                             start=(k == 0), stop=(k == DC - 1))
                _mm.then_inc(tn.h, 1)
            tn.bump()

            @VE
            def _(v, tgt=tn.n):
                v.wait_ge(tn.h, tgt)
                v.tensor_copy(lgE, rr(PB[0:E, 1536:1536 + TS]))
                v.tensor_copy(vspace[:], zeros8[:]).then_inc(vc.h, 1)
            vc.bump()

            for tc in range(4):
                @TE
                def _(t, tc=tc, tgt=vc.n):
                    t.wait_ge(vc.h, tgt)
                    _mm = t.transpose(rr(PB[:, 1024:1024 + E]), lgE[:, tc * 128:(tc + 1) * 128], identr[0:E, 0:E])
                    _mm.then_inc(tn.h, 1)
                tn.bump()

                @VE
                def _(v, tgt=tn.n):
                    v.wait_ge(tn.h, tgt)
                    v.tensor_copy(lgt[:], PB[:, 1024:1024 + E])
                    v.engine_nop()
                    v.engine_nop()
                    v.engine_nop()
                    v.engine_nop()
                    v.reduce_max(tp2[:, 0:1], lgt[:], axis=AXX, negate=True)
                    v.engine_nop()
                    v.engine_nop()
                    v.engine_nop()
                    v.engine_nop()
                    v.scalar_tensor_tensor(out=tp8[:, 0:8], in0=lgt[:], scalar=tp2[:, 0:1],
                                           in1=zeros8[:], op0=ALU.add, op1=ALU.add)
                    v.engine_nop()
                    v.engine_nop()
                    v.engine_nop()
                    v.engine_nop()
                    v.tensor_scalar(out=tpeq[:, 0:8], in0=tp8[:, 0:8], scalar1=0.0, scalar2=None,
                                    op0=ALU.is_equal)
                    v.engine_nop()
                    v.engine_nop()
                    v.engine_nop()
                    v.engine_nop()
                    v.tensor_copy(tp8[:, 8:16], tpeq[:, 0:8])
                    v.engine_nop()
                    v.engine_nop()
                    v.engine_nop()
                    v.engine_nop()
                    v.scalar_tensor_tensor(out=tp8[:, 16:24], in0=tp8[:, 8:16], scalar=-1e30,
                                           in1=lgt[:], op0=ALU.mult, op1=ALU.add)
                    v.engine_nop()
                    v.engine_nop()
                    v.engine_nop()
                    v.engine_nop()
                    v.reduce_max(tp2[:, 1:2], tp8[:, 16:24], axis=AXX, negate=True)
                    v.engine_nop()
                    v.engine_nop()
                    v.engine_nop()
                    v.engine_nop()
                    v.scalar_tensor_tensor(out=tp8[:, 24:32], in0=tp8[:, 16:24], scalar=tp2[:, 1:2],
                                           in1=zeros8[:], op0=ALU.add, op1=ALU.add)
                    v.engine_nop()
                    v.engine_nop()
                    v.engine_nop()
                    v.engine_nop()
                    v.tensor_scalar(out=tpeq[:, 8:16], in0=tp8[:, 24:32], scalar1=0.0, scalar2=None,
                                    op0=ALU.is_equal)
                    v.engine_nop()
                    v.engine_nop()
                    v.engine_nop()
                    v.engine_nop()
                    v.tensor_copy(tp8[:, 32:40], tpeq[:, 8:16])
                    v.tensor_sub(tp2[:, 2:3], tp2[:, 1:2], tp2[:, 0:1])
                    v.tensor_copy(vspace[:], zeros8[:]).then_inc(vc.h, 1)
                vc.bump()

                @SE
                def _(s, tgt=vc.n):
                    s.wait_ge(vc.h, tgt)
                    s.activation(out=tp2[:, 3:4], in_=tp2[:, 2:3], func=AF.Sigmoid)
                    s.activation(out=sspace[:], in_=zeros8[:], func=AF.Identity).then_inc(sc.h, 1)
                sc.bump()

                @VE
                def _(v, tgt=sc.n):
                    v.wait_ge(sc.h, tgt)
                    v.tensor_scalar(out=tp2[:, 4:5], in0=tp2[:, 3:4], scalar1=-1.0, scalar2=1.0,
                                    op0=ALU.mult, op1=ALU.add)
                    v.engine_nop()
                    v.engine_nop()
                    v.engine_nop()
                    v.engine_nop()
                    v.scalar_tensor_tensor(out=tp8[:, 0:8], in0=tp8[:, 8:16], scalar=tp2[:, 3:4],
                                           in1=zeros8[:], op0=ALU.mult, op1=ALU.add)
                    v.engine_nop()
                    v.engine_nop()
                    v.engine_nop()
                    v.engine_nop()
                    v.scalar_tensor_tensor(out=w8b, in0=tp8[:, 32:40], scalar=tp2[:, 4:5],
                                           in1=tp8[:, 0:8], op0=ALU.mult, op1=ALU.add)
                    v.tensor_copy(vspace[:], zeros8[:]).then_inc(vc.h, 1)
                vc.bump()

                @TE
                def _(t, tc=tc, tgt=vc.n):
                    t.wait_ge(vc.h, tgt)
                    _mm = t.transpose(rr(PB[0:E, 1280:1280 + 128]), w8b, identr[:])
                    _mm.then_inc(tn.h, 1)
                tn.bump()

                @VE
                def _(v, tc=tc, tgt=tn.n):
                    v.wait_ge(tn.h, tgt)
                    v.tensor_copy(w8T[:, tc * 128:(tc + 1) * 128], rr(PB[0:E, 1280:1280 + 128]))
                    v.tensor_copy(vspace[:], zeros8[:]).then_inc(vc.h, 1)
                vc.bump()

            for e_ in range(E):
                @TE
                def _(t, e_=e_, tgt=vc.n):
                    t.wait_ge(vc.h, tgt)
                    _mm = t.matmul(PB[:, 1536:1536 + TS], sel8[:, e_ * 128:(e_ + 1) * 128], w8T[:],
                             start=True, stop=True)
                    _mm.then_inc(tn.h, 1)
                tn.bump()

                @VE
                def _(v, e_=e_, tgt=tn.n):
                    v.wait_ge(tn.h, tgt)
                    v.tensor_copy(Wb[:, e_ * TS:(e_ + 1) * TS], PB[:, 1536:1536 + TS])
                    v.tensor_copy(vspace[:], zeros8[:]).then_inc(vc.h, 1)
                vc.bump()


        if DEBUG and ts == 0:
            @GE
            def _(g, tgt=vc.n):
                g.wait_ge(vc.h, tgt)
                g.dma_start(out=wbdbg_s[:], in_=rr(Wb)).then_inc(gd.h, 16)
                g.dma_start(out=ffdbg_s[6, 0:E, :], in_=w8T).then_inc(gd.h, 16)
                g.dma_start(out=ffdbg_s[7, 0:E, :], in_=lgE).then_inc(gd.h, 16)
                g.dma_start(out=ffdbg_s[5, :, 0:40], in_=rr(tp8[:])).then_inc(gd.h, 16)
                g.dma_start(out=ffdbg_s[5, :, 64:74], in_=rr(tp2[:])).then_inc(gd.h, 16)
                g.dma_start(out=ffdbg_s[5, :, 80:88], in_=rr(w8b)).then_inc(gd.h, 16)
            gd.bump(96)

        if EF_SUB >= 3:
            for e_ in range(min(E, T13_N)):
                for (at, tcb) in ((a1t, t1c[e_]), (a3t, t3c[e_])):
                    @TE
                    def _(t, at=at, e_=e_, tgt=vc.n):
                        t.wait_ge(vc.h, tgt)
                        for k in range(DC):
                            _mm = t.matmul(PB[0:R, 1536:1536 + TS],
                                     at[:, k * 128 + e_ * R:k * 128 + (e_ + 1) * R],
                                     xl2t_v[:, k, :], start=(k == 0), stop=(k == DC - 1))
                        _mm.then_inc(tn.h, 1)
                    tn.bump()

                    @VE
                    def _(v, tcb=tcb, tgt=tn.n):
                        v.wait_ge(tn.h, tgt)
                        v.tensor_scalar(out=tcb, in0=PB[0:R, 1536:1536 + TS], scalar1=2.0, scalar2=None,
                                        op0=ALU.mult)
                        v.tensor_copy(vspace[:], zeros8[:]).then_inc(vc.h, 1)
                    vc.bump()

        @TE
        def _(t, tgt=vc.n):
            t.wait_ge(vc.h, tgt)
            _mm = t.matmul(PB[:, 512:528], ident32[:], zerosT[:, 0:16], start=True, stop=True,
                           skip_group_check=True)
            _mm.then_inc(tn.h, 1)
        tn.bump()

        e_use = [tn.n, tn.n]
        for fc in range(FC if EF_SUB >= 4 else 0):
            b = fc % 2

            @SY
            def _(e, fc=fc, b=b, tgt=e_use[b]):
                e.wait_ge(tn.h, tgt)
                e.dma_start(out=w1s[b].rearrange("p (k f) -> p k f", k=DC),
                            in_=rr(w1T_b[fc]).rearrange("(k p) f -> p k f", p=128)).then_inc(wd.h, 16)
                e.dma_start(out=w3s[b].rearrange("p (k f) -> p k f", k=DC),
                            in_=rr(w3T_b[fc]).rearrange("(k p) f -> p k f", p=128)).then_inc(wd.h, 16)
                e.dma_start(out=b1s[b].rearrange("r (e f) -> r e f", e=E),
                            in_=rr(b1T_b[fc]).rearrange("e r f -> r e f")).then_inc(wd.h, 16)
                e.dma_start(out=b3s[b].rearrange("r (e f) -> r e f", e=E),
                            in_=rr(b3T_b[fc]).rearrange("e r f -> r e f")).then_inc(wd.h, 16)
                e.dma_start(out=a2s[b], in_=rr(a2T_b[fc])).then_inc(wd.h, 16)
            wd.bump(80)

            @TE
            def _(t, fc=fc, b=b, tgt=wd.n, tv=vc.n):
                t.wait_ge(wd.h, tgt)
                t.wait_ge(vc.h, tv)
                for k in range(DC):
                    t.matmul(PA[:, 0:512], w1s[b][:, k * 128:(k + 1) * 128], xl2t_v[:, k, :],
                             start=(k == 0), stop=(k == DC - 1))
                for k in range(DC):
                    _mm = t.matmul(PA[:, 512:1024], w3s[b][:, k * 128:(k + 1) * 128], xl2t_v[:, k, :],
                             start=(k == 0), stop=(k == DC - 1))
                _mm.then_inc(tn.h, 1)
            tn.bump()

            @VE
            def _(v, tgt=tn.n):
                v.wait_ge(tn.h, tgt)
                v.tensor_copy(base1sb, rr(PA[:, 0:512]))
                v.tensor_copy(base3sb, rr(PA[:, 512:1024]))
                v.tensor_copy(vspace[:], zeros8[:]).then_inc(vc.h, 1)
            vc.bump()

            h_sc = [sc.n, sc.n]
            h_vc = [vc.n, vc.n]

            for e_ in range(E):
                h1bk = PA[:, 1024:1536] if e_ % 2 == 0 else PB[:, 1024:1536]
                h3bk = PA[:, 1536:2048] if e_ % 2 == 0 else PB[:, 1536:2048]

                @TE
                def _(t, b=b, e_=e_, h1bk=h1bk, tgt=h_vc[e_ % 2], tsc=h_sc[e_ % 2]):
                    t.wait_ge(vc.h, tgt)
                    t.wait_ge(sc.h, tsc)
                    t.matmul(h1bk, identr[:], base1sb, start=True, stop=False,
                             skip_group_check=True)
                    _mm = t.matmul(h1bk, b1s[b][:, e_ * 128:(e_ + 1) * 128], t1c[e_],
                             start=False, stop=True, skip_group_check=True)
                    _mm.then_inc(tn.h, 1)
                tn.bump()

                @SE
                def _(s, h1bk=h1bk, tgt=tn.n):
                    s.wait_ge(tn.h, tgt)
                    s.activation(out=ssb, in_=h1bk, func=AF.Silu)
                    s.activation(out=sspace[:], in_=zeros8[:], func=AF.Identity).then_inc(sc.h, 1)
                sc.bump()

                @TE
                def _(t, fc=fc, b=b, e_=e_, h3bk=h3bk, tgt=h_vc[e_ % 2], tprev=vc.n):
                    t.wait_ge(vc.h, tgt)
                    _mm = t.matmul(h3bk, identr[:], base3sb, start=True, stop=False,
                                   skip_group_check=True)
                    _mm = t.matmul(h3bk, b3s[b][:, e_ * 128:(e_ + 1) * 128], t3c[e_],
                                   start=False, stop=True, skip_group_check=True)
                    if e_ > 0:
                        t.wait_ge(vc.h, tprev)
                        pbp = pwwsb[(e_ - 1) % 2]
                        _mm = t.matmul(PB[:, 0:512], identr[:], pbp, start=(e_ - 1 == 0), stop=False,
                                       skip_group_check=True)
                        for tc in range(4):
                            sl = ((e_ - 1) * 4 + tc) * R
                            _mm = t.matmul(PB[:, 512 + sl:512 + sl + R],
                                           pbp[:, tc * 128:(tc + 1) * 128], a2s[b][:, (e_ - 1) * R:e_ * R],
                                           start=False, stop=(fc == FC - 1), skip_group_check=True)
                    _mm.then_inc(tn.h, 1)
                tn.bump()

                @VE
                def _(v, e_=e_, h3bk=h3bk, tgt=tn.n, tsc=sc.n, pb=pwwsb[e_ % 2]):
                    v.wait_ge(tn.h, tgt)
                    v.wait_ge(sc.h, tsc)
                    v.tensor_tensor(pwsb, r32(ssb), h3bk, op=ALU.mult)
                    v.tensor_tensor(pb, r32(pwsb), r32(Wb[:, e_ * TS:(e_ + 1) * TS]), op=ALU.mult)
                    v.tensor_copy(vspace[:], zeros8[:]).then_inc(vc.h, 1)
                vc.bump()
                h_sc[e_ % 2] = sc.n
                h_vc[e_ % 2] = vc.n

            @TE
            def _(t, fc=fc, b=b, tgt=vc.n, pb=pwwsb[(E - 1) % 2]):
                t.wait_ge(vc.h, tgt)
                t.matmul(PB[:, 0:512], identr[:], pb, start=False, stop=True,
                         skip_group_check=True)
                for tc in range(4):
                    sl = ((E - 1) * 4 + tc) * R
                    _mm = t.matmul(PB[:, 512 + sl:512 + sl + R],
                             pb[:, tc * 128:(tc + 1) * 128], a2s[b][:, (E - 1) * R:E * R],
                             start=False, stop=(fc == FC - 1), skip_group_check=True)
                _mm.then_inc(tn.h, 1)
            tn.bump()

            if DEBUG and ts == 0 and fc == 0:
                @GE
                def _(g, tgt=vc.n, tsc=sc.n):
                    g.wait_ge(vc.h, tgt)
                    g.wait_ge(sc.h, tsc)
                    g.dma_start(out=ffdbg_s[0], in_=base1sb).then_inc(gd.h, 16)
                    g.dma_start(out=ffdbg_s[1], in_=base3sb).then_inc(gd.h, 16)
                    g.dma_start(out=ffdbg_s[2], in_=ssb).then_inc(gd.h, 16)
                    g.dma_start(out=ffdbg_s[3], in_=pwsb).then_inc(gd.h, 16)
                    g.dma_start(out=ffdbg_s[4], in_=pwwsb[(E - 1) % 2]).then_inc(gd.h, 16)
                    g.dma_start(out=ffdbg_s[5, 0:R, :], in_=t1c[0]).then_inc(gd.h, 16)
                gd.bump(96)

            e_use[b] = tn.n

            @VE
            def _(v, fc=fc, wb=waccsb[fc % 2], tgt=tn.n, tg=gd.n):
                v.wait_ge(tn.h, tgt)
                v.wait_ge(gd.h, tg)
                v.tensor_copy(wb, rr(PB[:, 0:512]))
                v.tensor_copy(vspace[:], zeros8[:]).then_inc(vc.h, 1)
            vc.bump()

            @GE
            def _(g, ts=ts, fc=fc, wb=waccsb[fc % 2], tgt=vc.n):
                g.wait_ge(vc.h, tgt)
                g.dma_start(out=wacc_s[ts, fc], in_=wb).then_inc(gd.h, 16)
            gd.bump(16)

        if EF_SUB >= 5:
            @VE
            def _(v, tgt=tn.n):
                v.wait_ge(tn.h, tgt)
                v.tensor_copy(usb, rr(PB[:, 512:1024]))
                v.tensor_copy(vspace[:], zeros8[:]).then_inc(vc.h, 1)
            vc.bump()

            for e_ in range(E):
                for tc in range(4):
                    slot = 1024 + ((e_ * 4 + tc) % 8) * 128

                    @TE
                    def _(t, e_=e_, tc=tc, slot=slot, tgt=vc.n):
                        t.wait_ge(vc.h, tgt)
                        _mm = t.transpose(rr(PB[0:R, slot:slot + 128]),
                                    usb[:, (e_ * 4 + tc) * R:(e_ * 4 + tc + 1) * R], identr[:])
                        _mm.then_inc(tn.h, 1)
                    tn.bump()

                    @VE
                    def _(v, e_=e_, tc=tc, slot=slot, tgt=tn.n):
                        v.wait_ge(tn.h, tgt)
                        v.tensor_scalar(out=ueT[:, e_ * TS + tc * 128:e_ * TS + (tc + 1) * 128],
                                        in0=rr(PB[0:R, slot:slot + 128]),
                                        scalar1=2.0, scalar2=None, op0=ALU.mult)
                        v.tensor_copy(vspace[:], zeros8[:]).then_inc(vc.h, 1)
                    vc.bump()


        if DEBUG and ts == 0:
            @GE
            def _(g, tgt=vc.n):
                g.wait_ge(vc.h, tgt)
                g.dma_start(out=wbdbg_s[0:R, :], in_=ueT).then_inc(gd.h, 16)
            gd.bump(16)

        if EF_SUB >= 6:
            @GE
            def _(g, ts=ts, tgt=vc.n, tsc=sc.n):
                g.wait_ge(vc.h, tgt)
                g.wait_ge(sc.h, tsc)
                g.dma_start(out=waccR_v, in_=wacc_s[ts].rearrange("k p t -> p k t")).then_inc(gd.h, 16)
            gd.bump(16)

            for dc in range(DC):
                @SY
                def _(e, dc=dc, tgt=tn.n):
                    e.wait_ge(tn.h, tgt)
                    e.dma_start(out=w2sF_v, in_=rr(w2T_b[dc]).rearrange("k p f -> p k f")).then_inc(wd.h, 16)
                    e.dma_start(out=b2sF.rearrange("r (e f) -> r e f", e=E),
                                in_=rr(b2T_b[dc]).rearrange("e r f -> r e f")).then_inc(wd.h, 16)
                wd.bump(32)

                pa_sl = PA[:, (dc % 2) * 512:(dc % 2) * 512 + 512]

                @TE
                def _(t, dc=dc, pa_sl=pa_sl, tgt=wd.n, tg=gd.n, tv=vc.n):
                    t.wait_ge(wd.h, tgt)
                    t.wait_ge(gd.h, tg)
                    t.wait_ge(vc.h, tv)
                    for k in range(FC):
                        t.matmul(pa_sl, w2sF_v[:, k, :], waccR_v[:, k, :],
                                 start=(k == 0), stop=False, skip_group_check=True)
                    for e_ in range(E):
                        _mm = t.matmul(pa_sl, b2sF[:, e_ * 128:(e_ + 1) * 128], ueT[:, e_ * TS:(e_ + 1) * TS],
                                 start=False, stop=(e_ == E - 1), skip_group_check=True)
                    _mm.then_inc(tn.h, 1)
                tn.bump()

                @VE
                def _(v, pa_sl=pa_sl, tgt=tn.n, tg=gd.n):
                    v.wait_ge(tn.h, tgt)
                    v.wait_ge(gd.h, tg)
                    v.tensor_copy(o2st, rr(pa_sl))
                    v.tensor_copy(vspace[:], zeros8[:]).then_inc(vc.h, 1)
                vc.bump()

                @GE
                def _(g, ts=ts, dc=dc, tgt=vc.n):
                    g.wait_ge(vc.h, tgt)
                    g.dma_start(out=out2_s[ts, dc], in_=o2st).then_inc(gd.h, 16)
                gd.bump(16)


        for tc in range(4 if EF_SUB >= 7 else 0):
            gtok = ts * 4 + tc

            @GE
            def _(g, ts=ts, tc=tc, gtok=gtok, tgt=vc.n):
                g.wait_ge(vc.h, tgt)
                g.dma_start(out=o2ld.rearrange("p (d f) -> p d f", d=DC),
                            in_=out2_s[ts, :, :, tc * 128:(tc + 1) * 128].rearrange("d p f -> p d f")).then_inc(gd.h, 16)
                g.dma_start(out=x2bF, in_=rr(x2_s[gtok * 128:(gtok + 1) * 128, :])).then_inc(gd.h, 16)
            gd.bump(32)

            @TE
            def _(t, tgt=gd.n, tv=vc.n, tsc=sc.n):
                t.wait_ge(gd.h, tgt)
                t.wait_ge(vc.h, tv)
                t.wait_ge(sc.h, tsc)
                for dcc in range(DC):
                    _mm = t.transpose(rr(PB[:, dcc * 128:(dcc + 1) * 128]),
                                o2ld[:, dcc * 128:(dcc + 1) * 128], identr[:])
                _mm.then_inc(tn.h, 1)
            tn.bump()

            @VE
            def _(v, tgt=tn.n):
                v.wait_ge(tn.h, tgt)
                v.tensor_copy(o2tok, rr(PB[:]))
                v.tensor_copy(vspace[:], zeros8[:]).then_inc(vc.h, 1)
            vc.bump()

            ln_stats(r32(o2tok), D, sqF)
            ln_apply(r32(o2tok), lnO)

            @VE
            def _(v, tgt=sc.n, tg=gd.n):
                v.wait_ge(sc.h, tgt)
                v.wait_ge(gd.h, tg)
                v.tensor_add(lnO, r32(x2bF), r32(lnO))
                v.tensor_copy(vspace[:], zeros8[:]).then_inc(vc.h, 1)
            vc.bump()

            @GE
            def _(g, gtok=gtok, tgt=vc.n):
                g.wait_ge(vc.h, tgt)
                g.dma_start(out=out_t[gtok * 128:(gtok + 1) * 128, :], in_=r32(lnO)).then_inc(st.h, 16)
            st.bump(16)

    @GE
    def _(g, tgt=st.n):
        g.wait_ge(st.h, tgt)

    ctx.close()
    return nc


def _host_layouts(inp):
    f32 = np.float32
    asf = lambda a: np.ascontiguousarray(np.asarray(a, dtype=f32))
    cc = np.ascontiguousarray
    Wq = asf(inp["L0_attn_Wq"]); Wk = asf(inp["L0_attn_Wk"])
    Wv = asf(inp["L0_attn_Wv"]); Wo = asf(inp["L0_attn_Wo"])
    blk = lambda W: cc(W.reshape(DC, 128, DC, 128).transpose(2, 0, 1, 3))
    selz = np.zeros((H, H * 128), dtype=f32)
    for h in range(H):
        selz[h, h * 128:(h + 1) * 128] = 1.0
    sel8 = np.zeros((E, E * 128), dtype=f32)
    for e_ in range(E):
        sel8[e_, e_ * 128:(e_ + 1) * 128] = 1.0
    return {
        "wq_b": blk(Wq), "wk_b": blk(Wk), "wv_b": blk(Wv), "wo_b": blk(Wo),
        "bqB": cc(asf(inp["L0_attn_bq"]).reshape(DC, 128).T),
        "bkB": cc(asf(inp["L0_attn_bk"]).reshape(DC, 128).T),
        "bvB": cc(asf(inp["L0_attn_bv"]).reshape(DC, 128).T),
        "boB": cc(asf(inp["L0_attn_bo"]).reshape(DC, 128).T),
        "gatewT": cc(asf(inp["L0_ffn_gate_w"]).T.reshape(DC, 128, E)),
        "w1T_b": cc(asf(inp["L0_ffn_w1"]).T.reshape(D, FC, 128).transpose(1, 0, 2)),
        "w3T_b": cc(asf(inp["L0_ffn_w3"]).T.reshape(D, FC, 128).transpose(1, 0, 2)),
        "w2T_b": cc(asf(inp["L0_ffn_w2"]).T.reshape(FC, 128, DC, 128).transpose(2, 0, 1, 3)),
        "a1T_b": cc(asf(inp["L0_ffn_a1"]).transpose(2, 0, 1).reshape(D, 128).reshape(DC, 128, 128)),
        "a3T_b": cc(asf(inp["L0_ffn_a3"]).transpose(2, 0, 1).reshape(D, 128).reshape(DC, 128, 128)),
        "a2T_b": cc(asf(inp["L0_ffn_a2"]).transpose(2, 0, 1).reshape(FF, 128).reshape(FC, 128, 128)),
        "b1T_b": cc(asf(inp["L0_ffn_b1"]).reshape(E, FC, 128, R).transpose(1, 0, 3, 2)),
        "b3T_b": cc(asf(inp["L0_ffn_b3"]).reshape(E, FC, 128, R).transpose(1, 0, 3, 2)),
        "b2T_b": cc(asf(inp["L0_ffn_b2"]).reshape(E, DC, 128, R).transpose(1, 0, 3, 2)),
        "identin": np.eye(128, dtype=f32),
        "maskin": np.triu(np.ones((128, 128), dtype=f32)),
        "selzin": selz,
        "sel8in": sel8,
        "onecolin": np.ones((128, 1), dtype=f32),
    }


def kernel(**inputs):
    shared = _host_layouts(inputs)
    x = np.ascontiguousarray(np.asarray(inputs["L0_attn_x"], dtype=np.float32).reshape(SEQ, D))
    nc = build_nc()
    in_maps = []
    for c in range(NCORE):
        m = dict(shared)
        m["x"] = np.ascontiguousarray(x[c * NTOK:(c + 1) * NTOK])
        pm = np.zeros((128, E), np.float32)
        pm[:, :c] = 1.0
        m["prefin"] = pm
        in_maps.append(m)
    res = run_bass_kernel_spmd(nc, in_maps, core_ids=list(range(NCORE)))
    out = np.concatenate([r["out"] for r in res.results], axis=0)
    return out.reshape(1, SEQ, D)
